# version 01 (final)
# baseline (speedup 1.0000x reference)
import math
import numpy as np
import ml_dtypes
from contextlib import ExitStack
from collections import defaultdict

import concourse.bass as bass
import concourse.mybir as mybir
from concourse.alu_op_type import AluOpType
from concourse.bass_utils import run_bass_kernel_spmd

F32 = mybir.dt.float32
BF16 = mybir.dt.bfloat16
I32 = mybir.dt.int32
AF = mybir.ActivationFunctionType
OP = AluOpType

NCORES = 8
T = 1024
D = 2048
DC = 16
HK, HV, DK, DV = 16, 32, 128, 128
CHUNK = 128
NCH = T // CHUNK
FFN_H = 5632
FFN_HC = 44
EPS_LN = 1e-5
EPS_L2 = 1e-6
EPS_RMS = 1e-6
EPS_DEN = 1e-6
NEG = -30000.0

ENGINES = ("tensor", "vector", "scalar", "gpsimd", "sync")


class Prog:

    def __init__(self, nc, stack):
        self.nc = nc
        self.stack = stack
        self.streams = {e: [] for e in ENGINES}
        self.sems = {}
        self.counts = defaultdict(int)
        self.seq = 0
        self.tok_seq = {}

    def sem(self, name):
        if name not in self.sems:
            self.sems[name] = self.stack.enter_context(self.nc.semaphore(name))
        return name

    def op(self, eng, emit, waits=(), inc=None, inc_amt=1):
        waits = tuple(t for t in waits if t is not None)
        for (s, v) in waits:
            assert s in self.sems, f"wait on undeclared sem {s}"
            assert v <= self.counts[s], f"wait on future value {s}={v} (now {self.counts[s]})"
        tok = None
        if inc is not None:
            self.sem(inc)
            self.counts[inc] += inc_amt
            tok = (inc, self.counts[inc])
        self.streams[eng].append((emit, waits, inc, inc_amt))
        self.seq += 1
        return tok

    def dma(self, eng, emit, sem, waits=()):
        self.sem(sem)
        return self.op(eng, emit, waits, inc=sem, inc_amt=16)

    def barrier(self, name):
        toks = []
        for e in ENGINES:
            toks.append(self.op(e, lambda eng: eng.engine_nop()
                                if hasattr(eng, "engine_nop") else eng.nop(),
                                inc=self.sem(f"bar_{e}")))
        for e in ENGINES:
            self.op(e, None, waits=[t for t in toks])
        return toks

    def emit(self, block):
        sems = self.sems
        for ename in ENGINES:
            stream = self.streams[ename]

            def body(eng, stream=stream, ename=ename):
                marks = defaultdict(int)
                own = None
                for (emit, waits, inc, inc_amt) in stream:
                    for (s, v) in waits:
                        if marks[s] < v:
                            eng.wait_ge(sems[s], v)
                            marks[s] = v
                    if emit is None:
                        continue
                    inst = emit(eng)
                    if inc is not None:
                        inst.then_inc(sems[inc], inc_amt)
            getattr(block, ename)(body)



def host_consts(core_id):
    bf = ml_dtypes.bfloat16
    c = {}
    c["id_bf"] = np.eye(128, dtype=bf)
    c["id_f32"] = np.eye(128, dtype=np.float32)
    c["ind32_f32"] = np.eye(32, dtype=np.float32)
    c["ind32_bf"] = np.eye(32, dtype=bf)
    c["ind16_bf"] = np.eye(16, dtype=bf)
    c["ones_bf"] = np.ones((128, 512), dtype=bf)
    c["ones_f32"] = np.ones((128, 16), dtype=np.float32)
    c["onesrow_f32"] = np.ones((1, 128), dtype=np.float32)
    i = np.arange(CHUNK)
    blk = i // 32
    same = blk[:, None] == blk[None, :]
    mD = np.where(same & (i[:, None] > i[None, :]), 0.0, NEG)
    mDT = np.where(same & (i[None, :] > i[:, None]), 0.0, NEG)
    mLT = np.where(blk[None, :] > blk[:, None], 0.0, NEG)
    mNT = np.where(i[None, :] >= i[:, None], 0.0, NEG)
    c["masksE"] = np.concatenate([mD, mDT, mLT, mNT], axis=1).astype(np.float32)
    indpair = np.zeros((16, 128, 32), dtype=bf)
    for kh in range(16):
        indpair[kh, :, 2 * kh] = 1
        indpair[kh, :, 2 * kh + 1] = 1
    c["indpair16"] = indpair.transpose(1, 0, 2).reshape(128, 16 * 32).copy()
    indsel16 = np.zeros((16, 128, 16), dtype=bf)
    for h in range(16):
        indsel16[h, :, h] = 1
    c["indsel16"] = indsel16.transpose(1, 0, 2).reshape(128, 16 * 16).copy()
    indsel32 = np.zeros((32, 128, 32), dtype=bf)
    for h in range(32):
        indsel32[h, :, h] = 1
    c["indsel32"] = indsel32.transpose(1, 0, 2).reshape(128, 32 * 32).copy()
    sel2 = np.zeros((128, 4), dtype=bf)
    sel2[:, 0] = 1
    sel2[:, 3] = 1
    c["sel2"] = sel2
    mk = np.zeros((128, 8), dtype=np.float32)
    mk[:, :core_id] = 1.0
    c["coremask"] = mk
    pm = np.zeros((128, 8), dtype=np.float32)
    if core_id > 0:
        pm[:, core_id - 1] = 1.0
    c["prevmask"] = pm
    c["halomask"] = np.full((128, 1), 0.0 if core_id == 0 else 1.0, np.float32)
    cc = np.zeros((128, 8), np.float32)
    cc[:, 0] = EPS_LN; cc[:, 1] = 1.0; cc[:, 2] = EPS_RMS; cc[:, 3] = 0.0
    cc[:, 4] = EPS_DEN
    c["constcol"] = cc
    return c


CONST_SPECS = {
    "id_bf": ([128, 128], BF16), "id_f32": ([128, 128], F32),
    "ind32_f32": ([32, 32], F32), "ind32_bf": ([32, 32], BF16),
    "ind16_bf": ([16, 16], BF16),
    "ones_bf": ([128, 512], BF16), "ones_f32": ([128, 16], F32),
    "onesrow_f32": ([1, 128], F32),
    "masksE": ([128, 512], F32),
    "indpair16": ([128, 512], BF16), "indsel16": ([128, 256], BF16),
    "indsel32": ([128, 1024], BF16), "sel2": ([128, 4], BF16),
    "coremask": ([128, 8], F32), "halomask": ([128, 1], F32),
    "prevmask": ([128, 8], F32),
    "constcol": ([128, 8], F32),
}



def build_nc(phases=("attn_a", "ffn1", "attn_b", "ffn2"), ab_stop="full"):
    nc = bass.Bass("TRN2", target_bir_lowering=False)
    d = {}
    d["x_tok"] = nc.dram_tensor("x_tok", [1024, 2048], F32, kind="ExternalInput").ap()
    d["w_qkv"] = nc.dram_tensor("w_qkv", [2048, 6144], F32, kind="ExternalInput").ap()
    d["w_out_a"] = nc.dram_tensor("w_out_a", [2048, 2048], F32, kind="ExternalInput").ap()
    d["decay_p"] = nc.dram_tensor("decay_p", [16, 1], F32, kind="ExternalInput").ap()
    d["wqkvzT"] = nc.dram_tensor("wqkvzT", [2048, 12288], F32, kind="ExternalInput").ap()
    d["wbaT"] = nc.dram_tensor("wbaT", [2048, 64], F32, kind="ExternalInput").ap()
    d["woutbT"] = nc.dram_tensor("woutbT", [4096, 2048], F32, kind="ExternalInput").ap()
    d["conv_w"] = nc.dram_tensor("conv_w", [128, 256], F32, kind="ExternalInput").ap()
    d["dt_bias"] = nc.dram_tensor("dt_bias", [32, 1], F32, kind="ExternalInput").ap()
    d["A_log"] = nc.dram_tensor("A_log", [32, 1], F32, kind="ExternalInput").ap()
    d["norm_w"] = nc.dram_tensor("norm_w", [128, 1], F32, kind="ExternalInput").ap()
    d["wqgT"] = nc.dram_tensor("wqgT", [2048, 5632], I32, kind="ExternalInput").ap()
    d["wquT"] = nc.dram_tensor("wquT", [2048, 5632], I32, kind="ExternalInput").ap()
    d["wqdT"] = nc.dram_tensor("wqdT", [5632, 2048], I32, kind="ExternalInput").ap()
    d["s_gate"] = nc.dram_tensor("s_gate", [128, 44], F32, kind="ExternalInput").ap()
    d["s_up"] = nc.dram_tensor("s_up", [128, 44], F32, kind="ExternalInput").ap()
    d["s_down"] = nc.dram_tensor("s_down", [128, 16], F32, kind="ExternalInput").ap()
    for k, (shape, dt) in CONST_SPECS.items():
        d[k] = nc.dram_tensor(k, shape, dt, kind="ExternalInput").ap()
    d["y"] = nc.dram_tensor("y", [1024, 2048], F32, kind="ExternalOutput").ap()
    import os as _os
    if _os.environ.get("AB_DUMP"):
        d["dbg"] = nc.dram_tensor("dbg", [64, 128, 1024], F32, kind="ExternalOutput").ap()

    d["xres"] = nc.dram_tensor("xres", [16, 128, 1024], F32).ap()
    d["fina"] = nc.dram_tensor("fina", [128, 32], F32)
    d["finall"] = nc.dram_tensor("finall", [1024, 32], F32)
    d["wg_bf"] = nc.dram_tensor("wg_bf", [44, 128, 2048], BF16).ap()
    d["wu_bf"] = nc.dram_tensor("wu_bf", [44, 128, 2048], BF16).ap()
    d["wd_bf"] = nc.dram_tensor("wd_bf", [16, 44, 128, 128], BF16).ap()
    d["zscr"] = nc.dram_tensor("zscr", [32, 128, 1024], BF16).ap()
    d["qkvscr"] = nc.dram_tensor("qkvscr", [64, 128, 1024], BF16).ap()
    d["bulkscr"] = nc.dram_tensor("bulkscr", [32, 8, 7, 128, 128], BF16).ap()
    d["vscr"] = nc.dram_tensor("vscr", [32, 8, 128, 128], BF16).ap()
    d["hscr"] = nc.dram_tensor("hscr", [32, 128, 1024], BF16).ap()
    d["hgscr"] = nc.dram_tensor("hgscr", [32, 128, 1024], BF16).ap()
    d["pub"] = nc.dram_tensor("pub", [32, 2, 128, 128], BF16)
    d["puball"] = nc.dram_tensor("puball", [256, 2, 128, 128], BF16)
    d["hal"] = nc.dram_tensor("hal", [128, 192], BF16)
    d["halall"] = nc.dram_tensor("halall", [1024, 192], BF16)

    with ExitStack() as stack:
        P = Prog(nc, stack)
        ctx = {"nc": nc, "d": d, "P": P}
        g = {}
        for k, (shape, dt) in CONST_SPECS.items():
            g[k] = stack.enter_context(nc.sbuf_tensor("c_" + k, shape, dt))
        g["zero4k"] = stack.enter_context(nc.sbuf_tensor("zero4k", [128, 1024], F32))
        g["smallp"] = stack.enter_context(nc.sbuf_tensor("smallp", [32, 8], F32))
        g["norm_w"] = stack.enter_context(nc.sbuf_tensor("s_norm_w", [128, 1], F32))
        g["sgate"] = stack.enter_context(nc.sbuf_tensor("sgate", [128, 44], F32))
        g["sup"] = stack.enter_context(nc.sbuf_tensor("sup", [128, 44], F32))
        g["sdown"] = stack.enter_context(nc.sbuf_tensor("sdown", [128, 16], F32))
        g["convw"] = stack.enter_context(nc.sbuf_tensor("convw", [128, 256], F32))
        psum = {
            "big": [stack.enter_context(nc.psum_tensor(f"pb{i}", [128, 512], F32)) for i in range(4)],
            "small": [stack.enter_context(nc.psum_tensor(f"ps{i}", [128, 128], F32)) for i in range(2)],
            "bf": [stack.enter_context(nc.psum_tensor(f"pt{i}", [128, 128], BF16)) for i in range(2)],
        }
        ctx.update(g=g, psum=psum, stack=stack)
        for i, v in enumerate([EPS_LN, 1.0, EPS_RMS, 0.0, EPS_DEN]):
            nc.const_aps.register(F32, float(v), g["constcol"][:, i:i + 1])

        tload = []
        for k in CONST_SPECS:
            tload.append(P.dma("sync", (lambda kk: (lambda e: e.dma_start(out=g[kk][:], in_=d[kk])))(k), "cst"))
        tload.append(P.dma("sync", lambda e: e.dma_start(out=g["smallp"][0:16, 0:1], in_=d["decay_p"]), "cst"))
        tload.append(P.dma("sync", lambda e: e.dma_start(out=g["smallp"][0:32, 1:2], in_=d["dt_bias"]), "cst"))
        tload.append(P.dma("sync", lambda e: e.dma_start(out=g["smallp"][0:32, 2:3], in_=d["A_log"]), "cst"))
        tload.append(P.dma("sync", lambda e: e.dma_start(out=g["norm_w"][:], in_=d["norm_w"]), "cst"))
        tload.append(P.dma("sync", lambda e: e.dma_start(out=g["sgate"][:], in_=d["s_gate"]), "cst"))
        tload.append(P.dma("sync", lambda e: e.dma_start(out=g["sup"][:], in_=d["s_up"]), "cst"))
        tload.append(P.dma("sync", lambda e: e.dma_start(out=g["sdown"][:], in_=d["s_down"]), "cst"))
        tload.append(P.dma("sync", lambda e: e.dma_start(out=g["convw"][:], in_=d["conv_w"]), "cst"))
        tz = P.op("vector", lambda e: e.memset(g["zero4k"][:], 0.0), inc="vE")
        ctx["tconst"] = tload[-1]
        ctx["tzero"] = tz

        emit_phase0(ctx)
        if "prepass" not in ctx:
            pass
        if "attn_a" in phases:
            emit_attn_a(ctx)
        if "ffn1" in phases:
            emit_ffn(ctx, tag="f1")
        if "attn_b" in phases:
            emit_attn_b(ctx, ab_stop)
        if "ffn2" in phases:
            emit_ffn(ctx, tag="f2")
        emit_out(ctx)

        with nc.Block() as block:
            P.emit(block)
    return nc


def emit_phase0(ctx):
    P, nc, d, g, stack = ctx["P"], ctx["nc"], ctx["d"], ctx["g"], ctx["stack"]
    psum = ctx["psum"]
    with ExitStack() as sc:
        xtok = [sc.enter_context(nc.sbuf_tensor(f"p0xt{i}", [128, 2048], F32)) for i in range(2)]
        xw = sc.enter_context(nc.sbuf_tensor("p0xw", [128, 16384], F32))
        tld = [None, None]
        tuse = [None, None]
        tev = None
        for tt in range(8):
            b = tt % 2
            tld[b] = P.dma("sync", lambda e, b=b, tt=tt: e.dma_start(
                out=xtok[b][:], in_=d["x_tok"][tt * 128:(tt + 1) * 128, :]),
                f"p0x{b}", waits=[tuse[b], ctx["tconst"]])
            for dc in range(16):
                pb = psum["small"][dc % 2]
                ttr = P.op("tensor", lambda e, b=b, dc=dc, pb=pb: e.transpose(
                    pb[:], xtok[b][:, dc * 128:(dc + 1) * 128], g["id_f32"][:]),
                    waits=[tld[b], tev], inc="tE")
                tuse[b] = ttr
                tev = P.op("vector", lambda e, dc=dc, tt=tt, pb=pb: e.tensor_copy(
                    xw[:, dc * 1024 + tt * 128: dc * 1024 + tt * 128 + 128], pb[:]),
                    waits=[ttr], inc="vE")
        twr = []
        for dc in range(16):
            twr.append(P.dma("sync", lambda e, dc=dc: e.dma_start(
                out=d["xres"][dc], in_=xw[:, dc * 1024:(dc + 1) * 1024]),
                f"p0w{dc % 4}", waits=[tev]))
        ctx["t_xres"] = twr


def emit_ln(ctx, sc, xln, name, waits):
    P, nc, d, g, psum = ctx["P"], ctx["nc"], ctx["d"], ctx["g"], ctx["psum"]
    xt = [sc.enter_context(nc.sbuf_tensor(f"{name}xt{i}", [128, 512], F32)) for i in range(2)]
    xb = sc.enter_context(nc.sbuf_tensor(f"{name}xb", [128, 512], BF16))
    sq = sc.enter_context(nc.sbuf_tensor(f"{name}sq", [128, 512], BF16))
    rows = sc.enter_context(nc.sbuf_tensor(f"{name}rw", [1, 1024], F32))
    rowbf = sc.enter_context(nc.sbuf_tensor(f"{name}rb", [1, 2048], BF16))
    tmp = sc.enter_context(nc.sbuf_tensor(f"{name}tp", [128, 512], F32))
    ctx["ln_tmp"] = tmp
    stats = ctx["psum"]["big"]
    tmm = [None, None]
    tuse = [None, None]
    tcast = None
    tsq = None
    for dc in range(16):
        for tt in range(2):
            b = (dc * 2 + tt) % 2
            tld = P.dma("sync", lambda e, dc=dc, tt=tt, b=b: e.dma_start(
                out=xt[b][:], in_=d["xres"][dc][:, tt * 512:(tt + 1) * 512]),
                "lnl%d" % b, waits=[tuse[b]] + list(waits))
            tcast = P.op("vector", lambda e, b=b: e.tensor_copy(xb[:], xt[b][:]),
                         waits=[tld, tsq, tmm[0], tmm[1]], inc="vE")
            tsq = P.op("scalar", lambda e: e.activation(sq[:], xb[:], AF.Square),
                       waits=[tcast], inc="sE")
            tmm[0] = P.op("tensor", lambda e, dc=dc, tt=tt: e.matmul(
                stats[2 * tt][0:1, :], g["ones_bf"][:, 0:1], xb[:],
                start=(dc == 0), stop=False), waits=[tcast], inc="tE")
            tmm[1] = P.op("tensor", lambda e, dc=dc, tt=tt: e.matmul(
                stats[2 * tt + 1][0:1, :], g["ones_bf"][:, 0:1], sq[:],
                start=(dc == 0), stop=(dc == 15)), waits=[tsq], inc="tE")
            tuse[b] = tmm[1]
    A = lambda tt=None: (rows[0:1, 0:1024] if tt is None
                         else rows[0:1, tt * 512:(tt + 1) * 512])
    R0 = rowbf[0:1, 0:1024]
    R1 = rowbf[0:1, 1024:2048]
    t = tmm[1]
    for tt in range(2):
        t = P.op("vector", lambda e, tt=tt: e.tensor_copy(A(tt), stats[2 * tt][0:1, :]),
                 waits=[t], inc="vE")
    t = P.op("vector", lambda e: e.tensor_scalar(A(), A(), 1.0 / D, None, OP.mult),
             waits=[t], inc="vE")
    t = P.op("vector", lambda e: e.tensor_tensor(R0, A(), A(), OP.mult),
             waits=[t], inc="vE")
    for tt in range(2):
        t = P.op("vector", lambda e, tt=tt: e.tensor_scalar(
            stats[2 * tt + 1][0:1, :], stats[2 * tt + 1][0:1, :], 1.0 / D, None,
            OP.mult), waits=[t], inc="vE")
        t = P.op("vector", lambda e, tt=tt: e.tensor_tensor(
            stats[2 * tt + 1][0:1, :], stats[2 * tt + 1][0:1, :],
            R0[:, tt * 512:(tt + 1) * 512], OP.subtract), waits=[t], inc="vE")
        t = P.op("scalar", lambda e, tt=tt: e.activation(
            R1[:, tt * 512:(tt + 1) * 512], stats[2 * tt + 1][0:1, :],
            AF.Sqrt, bias=EPS_LN), waits=[t], inc="sE")
    def _recip(e, nc=nc):
        with nc.allow_low_precision(reason="bf16 LN rinv within tolerance"):
            return e.reciprocal(R0, R1)
    t = P.op("vector", _recip, waits=[t], inc="vE")
    t = P.op("vector", lambda e: e.tensor_tensor(R1, A(), R0, OP.mult),
             waits=[t], inc="vE")
    tout = None
    for tt in range(2):
        pr, pm = psum["big"][2], psum["big"][3]
        tb1 = P.op("tensor", lambda e, tt=tt, pr=pr: e.matmul(
            pr[:], g["ones_bf"][0:1, 0:128],
            rowbf[0:1, tt * 512:(tt + 1) * 512], start=True, stop=True),
            waits=[t, tout], inc="tE")
        tb2 = P.op("tensor", lambda e, tt=tt, pm=pm: e.matmul(
            pm[:], g["ones_bf"][0:1, 0:128],
            rowbf[0:1, 1024 + tt * 512: 1024 + (tt + 1) * 512], start=True, stop=True),
            waits=[t, tout], inc="tE")
        for dc in range(16):
            b = dc % 2
            tld = P.dma("sync", lambda e, dc=dc, tt=tt, b=b: e.dma_start(
                out=xt[b][:], in_=d["xres"][dc][:, tt * 512:(tt + 1) * 512]),
                "lnm%d" % b, waits=[tuse[b]])
            t1 = P.op("vector", lambda e, b=b, pr=pr: e.tensor_tensor(
                tmp[:], xt[b][:], pr[:], OP.mult), waits=[tld, tb1], inc="vE")
            tout = P.op("vector", lambda e, dc=dc, tt=tt, pm=pm: e.tensor_tensor(
                xln[:, dc * 1024 + tt * 512: dc * 1024 + (tt + 1) * 512],
                tmp[:], pm[:], OP.subtract), waits=[t1, tb2], inc="vE")
            tuse[b] = tout
    return tout


def emit_attn_a(ctx):
    P, nc, d, g, psum = ctx["P"], ctx["nc"], ctx["d"], ctx["g"], ctx["psum"]
    with ExitStack() as sc:
        xln = sc.enter_context(nc.sbuf_tensor("aa_xln", [128, 16384], BF16))
        q = sc.enter_context(nc.sbuf_tensor("aa_q", [128, 16384], BF16))
        kvl = sc.enter_context(nc.sbuf_tensor("aa_kv", [128, 16384], BF16))
        ksl = sc.enter_context(nc.sbuf_tensor("aa_ks", [128, 16384], BF16))
        wbuf = [sc.enter_context(nc.sbuf_tensor(f"aa_w{i}", [128, 4096], BF16)) for i in range(2)]
        kbuf = [sc.enter_context(nc.sbuf_tensor(f"aa_k{i}", [128, 1024], BF16)) for i in range(4)]
        vbuf = [sc.enter_context(nc.sbuf_tensor(f"aa_v{i}", [128, 1024], BF16)) for i in range(4)]
        ubuf = sc.enter_context(nc.sbuf_tensor("aa_u", [128, 512], BF16))
        ebuf = sc.enter_context(nc.sbuf_tensor("aa_e", [128, 512], BF16))
        scr = sc.enter_context(nc.sbuf_tensor("aa_scr", [128, 1024], F32))
        fin = sc.enter_context(nc.sbuf_tensor("aa_fin", [128, 32], F32))
        fsb = sc.enter_context(nc.sbuf_tensor("aa_fsb", [128, 256], F32))
        initp = sc.enter_context(nc.sbuf_tensor("aa_init", [128, 32], F32))
        tmb = sc.enter_context(nc.sbuf_tensor("aa_tmb", [128, 32], F32))
        tmc = sc.enter_context(nc.sbuf_tensor("aa_tmc", [128, 32], F32))
        dcol = sc.enter_context(nc.sbuf_tensor("aa_dcol", [128, 16], F32))
        dpow = sc.enter_context(nc.sbuf_tensor("aa_dpow", [128, 16], F32))
        dpow32 = sc.enter_context(nc.sbuf_tensor("aa_dpow32", [128, 32], F32))
        dec = sc.enter_context(nc.sbuf_tensor("aa_dec", [16, 2], F32))
        decrow = sc.enter_context(nc.sbuf_tensor("aa_decrow", [1, 16], F32))
        den = sc.enter_context(nc.sbuf_tensor("aa_den", [16, 512], F32))
        recbf = sc.enter_context(nc.sbuf_tensor("aa_rec", [16, 1024], BF16))
        xrt = [sc.enter_context(nc.sbuf_tensor(f"aa_xr{i}", [128, 512], F32)) for i in range(2)]

        t0 = P.op("scalar", lambda e: e.activation(dec[0:16, 0:1], g["smallp"][0:16, 0:1],
                  AF.Sigmoid), waits=[ctx["tconst"]], inc="sE")
        t0 = P.op("tensor", lambda e: e.transpose(psum["small"][0][0:1, 0:16],
                  dec[0:16, 0:1], g["id_f32"][0:16, 0:16]), waits=[t0] + ctx["t_xres"], inc="tE")
        t0 = P.op("vector", lambda e: e.tensor_copy(decrow[:], psum["small"][0][0:1, 0:16]),
                  waits=[t0], inc="vE")
        t0 = P.op("tensor", lambda e: e.matmul(psum["small"][1][:, 0:16],
                  g["onesrow_f32"][0:1, :], decrow[0:1, :], start=True, stop=True),
                  waits=[t0], inc="tE")
        t0 = P.op("vector", lambda e: e.tensor_copy(dcol[:], psum["small"][1][:, 0:16]),
                  waits=[t0], inc="vE")
        tln = P.op("scalar", lambda e: e.activation(dpow[:], dcol[:], AF.Ln),
                   waits=[t0], inc="sE")
        tdp = P.op("scalar", lambda e: e.activation(dpow[:], dpow[:], AF.Exp,
                   scale=float(T)), waits=[tln], inc="sE")
        tdp = P.op("vector", lambda e: e.tensor_copy(dpow32[:, 0:32:2], dpow[:]),
                   waits=[tdp], inc="vE")
        tdp = P.op("vector", lambda e: e.tensor_copy(dpow32[:, 1:32:2], dpow[:]),
                   waits=[tdp], inc="vE")

        tln1 = emit_ln(ctx, sc, xln, "ln1", ctx["t_xres"])
        rbuf = ctx["ln_tmp"]

        order = [(s * 256, "q", s) for s in range(8)]
        for i in range(8):
            order.append((2048 + i * 256, "k", i))
            order.append((4096 + i * 256, "v", i))
        tw = [None, None]
        twfree = [None, None]
        tpsum = [None, None, None, None]
        tevac = {}
        tscan = {}
        theadk = {}
        theadv = {}
        tphi3 = tphi4 = None
        tsc_last = None
        for si, (j0, kind, idx) in enumerate(order):
            b = si % 2
            tw[b] = P.dma("gpsimd", lambda e, j0=j0, b=b: e.dma_start(
                out=wbuf[b][:].rearrange("p (dc j) -> p dc j", dc=16),
                in_=d["w_qkv"][:, j0:j0 + 256].rearrange("(dc p) j -> p dc j", p=128)),
                f"aaw{b}", waits=[twfree[b], tln1])
            for jc in range(2):
                head = 2 * idx + jc
                for tt in range(2):
                    pp = (si * 4 + jc * 2 + tt) % 4
                    pb = psum["big"][pp]
                    tm = None
                    for dc in range(16):
                        tm = P.op("tensor", lambda e, b=b, dc=dc, jc=jc, tt=tt, pb=pb: e.matmul(
                            pb[:], wbuf[b][:, dc * 256 + jc * 128: dc * 256 + (jc + 1) * 128],
                            xln[:, dc * 1024 + tt * 512: dc * 1024 + (tt + 1) * 512],
                            start=(dc == 0), stop=(dc == 15)),
                            waits=[tw[b], tpsum[pp], (tln1 if dc == 0 else None)],
                            inc=("tE" if dc == 15 else None))
                    twfree[b] = tm
                    if kind in ("q", "k"):
                        dst = q if kind == "q" else kbuf[head % 4]
                        off = head * 1024 + tt * 512 if kind == "q" else tt * 512
                        t1 = P.op("scalar", lambda e, pb=pb: e.activation(
                            ubuf[:], pb[:], AF.Relu, scale=-1.0),
                            waits=[tm, tphi3, (theadk.get(head - 4) if kind == "k" else None)],
                            inc="sE")
                        t2 = P.op("scalar", lambda e: e.activation(
                            ebuf[:], ubuf[:], AF.Exp, scale=-1.0),
                            waits=[t1, tphi4], inc="sE")
                        t3 = P.op("vector", lambda e, pb=pb: e.tensor_tensor(
                            rbuf[:], pb[:], ubuf[:], OP.add), waits=[t1], inc="vE")
                        t4 = P.op("vector", lambda e, dst=dst, off=off: e.tensor_tensor(
                            dst[:, off:off + 512], ebuf[:], rbuf[:], OP.add),
                            waits=[t2, t3], inc="vE")
                        tphi3, tphi4 = t3, t4
                        tpsum[pp] = t3
                        tsc_last = t2
                        tevac[(kind, head, tt)] = t4
                        if kind == "k" and tt == 1:
                            theadk[head] = t4
                    else:
                        t4 = P.op("scalar", lambda e, head=head, tt=tt, pb=pb: e.activation(
                            vbuf[head % 4][:, tt * 512:(tt + 1) * 512], pb[:], AF.Copy),
                            waits=[tm, theadv.get(head - 4)], inc="sE")
                        tpsum[pp] = t4
                        tsc_last = t4
                        if tt == 1:
                            theadv[head] = t4
            if kind == "v":
                for jc in range(2):
                    head = 2 * idx + jc
                    kb, vb = kbuf[head % 4], vbuf[head % 4]
                    tp = P.op("vector", lambda e, head=head, kb=kb, vb=vb: e.tensor_tensor(
                        kvl[:, head * 1024:(head + 1) * 1024], kb[:], vb[:], OP.mult),
                        waits=[theadk[head], theadv[head],
                               tevac.get(("k", head, 0)), tdp], inc="vE")
                    ts1 = P.op("vector", lambda e, head=head: e.tensor_tensor_scan(
                        kvl[:, head * 1024:(head + 1) * 1024],
                        dcol[:, head:head + 1].broadcast_to((128, 1024)),
                        kvl[:, head * 1024:(head + 1) * 1024], 0.0,
                        OP.mult, OP.add), waits=[tp], inc="vE")
                    ts2 = P.op("vector", lambda e, head=head, kb=kb: e.tensor_tensor_scan(
                        ksl[:, head * 1024:(head + 1) * 1024],
                        dcol[:, head:head + 1].broadcast_to((128, 1024)),
                        kb[:], 0.0, OP.mult, OP.add), waits=[ts1], inc="vE")
                    tf1 = P.op("vector", lambda e, head=head: e.tensor_copy(
                        fin[:, 2 * head:2 * head + 1],
                        kvl[:, head * 1024 + 1023: head * 1024 + 1024]), waits=[ts1], inc="vE")
                    tf2 = P.op("vector", lambda e, head=head: e.tensor_copy(
                        fin[:, 2 * head + 1:2 * head + 2],
                        ksl[:, head * 1024 + 1023: head * 1024 + 1024]), waits=[ts2], inc="vE")
                    tscan[head] = tf2
                    theadk[head] = ts2
                    theadv[head] = tp

        tfw = P.dma("gpsimd", lambda e: e.dma_start(out=d["fina"].ap(), in_=fin[:]),
                    "aafin", waits=[tscan[h] for h in range(16)])
        tag = P.op("gpsimd", lambda e: e.collective_compute(
            "AllGather", OP.bypass, replica_groups=[list(range(NCORES))],
            ins=[d["fina"].ap().opt()], outs=[d["finall"].ap().opt()]),
            waits=[tfw], inc="ag1")
        tfr = P.dma("gpsimd", lambda e: e.dma_start(
            out=fsb[:].rearrange("p (r c) -> p r c", r=8),
            in_=d["finall"].ap().rearrange("(r p) c -> p r c", p=128)),
            "aafr", waits=[tag])
        ti = P.op("vector", lambda e: e.memset(initp[:], 0.0), waits=[tfr], inc="vE")
        dpow_ap = None
        for j in range(NCORES - 1):
            t1 = P.op("vector", lambda e, j=j: e.tensor_scalar(
                tmb[:], fsb[:, j * 32:(j + 1) * 32], g["coremask"][:, j:j + 1],
                None, OP.mult), waits=[ti], inc="vE")
            t2 = P.op("vector", lambda e: e.tensor_tensor(
                tmc[:], initp[:], dpow32[:], OP.mult), waits=[t1], inc="vE")
            ti = P.op("vector", lambda e: e.tensor_tensor(
                initp[:], tmc[:], tmb[:], OP.add), waits=[t2], inc="vE")

        tden = [None, None]
        for head in range(16):
            tpw = P.op("vector", lambda e, head=head: e.tensor_tensor_scan(
                scr[:], dcol[:, head:head + 1].broadcast_to((128, 1024)),
                g["zero4k"][:], 1.0, OP.mult, OP.add),
                waits=[ti, ctx["tzero"]], inc="vE")
            tc1 = P.op("vector", lambda e, head=head: e.scalar_tensor_tensor(
                kvl[:, head * 1024:(head + 1) * 1024], scr[:],
                initp[:, 2 * head:2 * head + 1],
                kvl[:, head * 1024:(head + 1) * 1024], OP.mult, OP.add),
                waits=[tpw], inc="vE")
            tc2 = P.op("vector", lambda e, head=head: e.scalar_tensor_tensor(
                ksl[:, head * 1024:(head + 1) * 1024], scr[:],
                initp[:, 2 * head + 1:2 * head + 2],
                ksl[:, head * 1024:(head + 1) * 1024], OP.mult, OP.add),
                waits=[tpw], inc="vE")
            for tt in range(2):
                buf = ubuf if tt == 0 else ebuf
                tq = P.op("vector", lambda e, head=head, tt=tt, buf=buf: e.tensor_tensor(
                    buf[:], q[:, head * 1024 + tt * 512: head * 1024 + (tt + 1) * 512],
                    ksl[:, head * 1024 + tt * 512: head * 1024 + (tt + 1) * 512], OP.mult),
                    waits=[tc2, tevac[("q", head, tt)], tden[tt]], inc="vE")
                tden[tt] = P.op("tensor", lambda e, head=head, tt=tt, buf=buf: e.matmul(
                    psum["big"][tt][0:16, :], g["indsel16"][:, head * 16:(head + 1) * 16],
                    buf[:], start=(head == 0), stop=(head == 15)),
                    waits=[tq] + ([tsc_last] if head == 0 else []), inc="tE")
            tscan[head] = tc1
        tr = None
        tclip = [None, None]
        for tt in range(2):
            t1 = P.op("vector", lambda e, tt=tt: e.tensor_scalar(
                den[:], psum["big"][tt][0:16, :],
                EPS_DEN, None, OP.max), waits=[tden[tt], tr], inc="vE")
            tclip[tt] = t1
            t2 = P.op("vector", lambda e, tt=tt: e.reciprocal(
                den[:], den[:]), waits=[t1], inc="vE")
            tr = P.op("vector", lambda e, tt=tt: e.tensor_copy(
                recbf[:, tt * 512:(tt + 1) * 512], den[:]),
                waits=[t2], inc="vE")
        tof = {}
        tpso = [tden[0], tden[1]]
        for head in range(16):
            for tt in range(2):
                buf = ubuf if tt == 0 else ebuf
                tq = P.op("vector", lambda e, head=head, tt=tt, buf=buf: e.tensor_tensor(
                    buf[:], q[:, head * 1024 + tt * 512: head * 1024 + (tt + 1) * 512],
                    kvl[:, head * 1024 + tt * 512: head * 1024 + (tt + 1) * 512], OP.mult),
                    waits=[tscan[head], tr, tpso[tt]], inc="vE")
                pp = psum["big"][2 + (tt % 2)]
                trb = P.op("tensor", lambda e, head=head, tt=tt, pp=pp: e.matmul(
                    pp[:], g["ind16_bf"][:, head:head + 1].broadcast_to((16, 128)),
                    recbf[0:16, tt * 512:(tt + 1) * 512], start=True, stop=True),
                    waits=[tr, tpso[tt]], inc="tE")
                t_of_mm_last = trb
                tof[(head, tt)] = P.op("vector", lambda e, head=head, tt=tt, pp=pp, buf=buf: e.tensor_tensor(
                    q[:, head * 1024 + tt * 512: head * 1024 + (tt + 1) * 512],
                    buf[:], pp[:], OP.mult),
                    waits=[trb, tq], inc="vE")
                tpso[tt] = tof[(head, tt)]

        txr = [None, None]
        tpsum = [tclip[0], tclip[1], tpso[0], tpso[1]]
        tw = [t_of_mm_last, t_of_mm_last]
        twfree = [t_of_mm_last, t_of_mm_last]
        ctx["t_aa"] = []
        ctx["t_last"] = ctx["t_aa"]
        for es in range(8):
            b = es % 2
            tw[b] = P.dma("gpsimd", lambda e, es=es, b=b: e.dma_start(
                out=wbuf[b][:].rearrange("p (dc j) -> p dc j", dc=16),
                in_=d["w_out_a"][:, es * 256:(es + 1) * 256].rearrange(
                    "(dc p) j -> p dc j", p=128)),
                f"aaow{b}", waits=[twfree[b]])
            for jc in range(2):
                ec = es * 2 + jc
                for tt in range(2):
                    pp = (ec * 2 + tt) % 4
                    pb = psum["big"][pp]
                    tm = None
                    for fc in range(16):
                        tm = P.op("tensor", lambda e, b=b, fc=fc, jc=jc, tt=tt, pb=pb: e.matmul(
                            pb[:], wbuf[b][:, fc * 256 + jc * 128: fc * 256 + (jc + 1) * 128],
                            q[:, fc * 1024 + tt * 512: fc * 1024 + (tt + 1) * 512],
                            start=(fc == 0), stop=(fc == 15)),
                            waits=[tw[b], tpsum[pp]] + ([tof[(fc, tt)]] if ec == 0 else []),
                            inc=("tE" if fc == 15 else None))
                    twfree[b] = tm
                    xb = (ec * 2 + tt) % 2
                    tld = P.dma("sync", lambda e, ec=ec, tt=tt, xb=xb: e.dma_start(
                        out=xrt[xb][:], in_=d["xres"][ec][:, tt * 512:(tt + 1) * 512]),
                        f"aax{xb}", waits=[txr[xb]])
                    ta = P.op("vector", lambda e, xb=xb, pb=pb: e.tensor_tensor(
                        xrt[xb][:], xrt[xb][:], pb[:], OP.add), waits=[tld, tm], inc="vE")
                    tpsum[pp] = ta
                    txr[xb] = P.dma("sync", lambda e, ec=ec, tt=tt, xb=xb: e.dma_start(
                        out=d["xres"][ec][:, tt * 512:(tt + 1) * 512], in_=xrt[xb][:]),
                        f"aaxw{xb}", waits=[ta])
                    ctx["t_aa"].append(txr[xb])



def emit_ffn(ctx, tag):
    P, nc, d, g, psum = ctx["P"], ctx["nc"], ctx["d"], ctx["g"], ctx["psum"]
    prev = ctx.get("t_last") or ctx["t_xres"]
    with ExitStack() as sc:
        xln = sc.enter_context(nc.sbuf_tensor(f"{tag}_xln", [128, 16384], BF16))
        wgt = [sc.enter_context(nc.sbuf_tensor(f"{tag}_wg{i}", [128, 2048], BF16)) for i in range(2)]
        wut = [sc.enter_context(nc.sbuf_tensor(f"{tag}_wu{i}", [128, 2048], BF16)) for i in range(2)]
        wdt = [sc.enter_context(nc.sbuf_tensor(f"{tag}_wd{i}", [128, 5632], BF16)) for i in range(2)]
        G = sc.enter_context(nc.sbuf_tensor(f"{tag}_G", [128, 22528], BF16))
        gt = [sc.enter_context(nc.sbuf_tensor(f"{tag}_gt{i}", [128, 512], BF16)) for i in range(2)]
        ut = [sc.enter_context(nc.sbuf_tensor(f"{tag}_ut{i}", [128, 512], BF16)) for i in range(2)]
        xrt = [sc.enter_context(nc.sbuf_tensor(f"{tag}_xr{i}", [128, 512], F32)) for i in range(2)]

        tln = emit_ln(ctx, sc, xln, f"ln{tag}", prev)
        ctx["t_last"] = []
        tw = {"g": [None, None], "u": [None, None], "d": [None, None]}
        twfree = {"g": [tln, tln], "u": [tln, tln], "d": [tln, tln]}
        tst = {"g": [None, None], "u": [None, None], "d": [None, None]}
        tpsum = [tln, tln, tln, tln]
        tgu = [None, None]
        txr = [None, None]
        conv_done = {"g": set(), "u": set(), "d": set()}
        convert = (tag == "f1")
        scrw = ctx.get("t_wscr", [])
        t_down_last = None
        for th in range(2):
            for hc in range(FFN_HC):
                b = hc % 2
                for kind, wt, src32, scr in (("g", wgt, d["wqgT"], d["wg_bf"]),
                                             ("u", wut, d["wquT"], d["wu_bf"])):
                    if convert and hc not in conv_done[kind]:
                        conv_done[kind].add(hc)
                        tld = P.dma("gpsimd", lambda e, hc=hc, b=b, wt=wt, src32=src32: e.dma_start(
                            out=wt[b][:].rearrange("p (dc h) -> p dc h", dc=16),
                            in_=src32[:, hc * 128:(hc + 1) * 128].rearrange(
                                "(dc p) h -> p dc h", p=128)),
                            "fn%sl%d" % (kind, b), waits=[twfree[kind][b], tst[kind][b]])
                        tts = P.op("vector", lambda e, b=b, wt=wt: e.tensor_scalar(
                            wt[b][:], wt[b][:], -128.0, None, OP.add),
                            waits=[tld], inc="vE")
                        tst[kind][b] = P.dma("sync", lambda e, hc=hc, b=b, wt=wt, scr=scr: e.dma_start(
                            out=scr[hc], in_=wt[b][:]), "fn%ss%d" % (kind, b), waits=[tts])
                        tw[kind][b] = tts
                    else:
                        tw[kind][b] = P.dma("sync", lambda e, hc=hc, b=b, wt=wt, scr=scr: e.dma_start(
                            out=wt[b][:], in_=scr[hc]), "fn%sl%d" % (kind, b),
                            waits=[twfree[kind][b], tst[kind][b]] + (scrw if hc < 2 else []))
                pg = psum["big"][0 + b]
                pu = psum["big"][2 + b]
                tmg = tmu = None
                for dc in range(16):
                    tmg = P.op("tensor", lambda e, b=b, dc=dc, th=th, pg=pg: e.matmul(
                        pg[:], wgt[b][:, dc * 128:(dc + 1) * 128],
                        xln[:, dc * 1024 + th * 512: dc * 1024 + (th + 1) * 512],
                        start=(dc == 0), stop=(dc == 15)),
                        waits=[tw["g"][b], (tpsum[0 + b] if dc == 0 else None),
                               (tln if dc == 0 else None)],
                        inc=("tE" if dc == 15 else None))
                for dc in range(16):
                    tmu = P.op("tensor", lambda e, b=b, dc=dc, th=th, pu=pu: e.matmul(
                        pu[:], wut[b][:, dc * 128:(dc + 1) * 128],
                        xln[:, dc * 1024 + th * 512: dc * 1024 + (th + 1) * 512],
                        start=(dc == 0), stop=(dc == 15)),
                        waits=[tw["u"][b], (tpsum[2 + b] if dc == 0 else None)],
                        inc=("tE" if dc == 15 else None))
                twfree["g"][b] = tmg
                twfree["u"][b] = tmu
                te1 = P.op("scalar", lambda e, hc=hc, b=b, pg=pg: e.activation(
                    gt[b][:], pg[:], AF.Silu, scale=g["sgate"][:, hc:hc + 1]),
                    waits=[tmg, tgu[b]], inc="sE")
                te2 = P.op("scalar", lambda e, hc=hc, b=b, pu=pu: e.activation(
                    ut[b][:], pu[:], AF.Copy, scale=g["sup"][:, hc:hc + 1]),
                    waits=[tmu, tgu[b]], inc="sE")
                tgu[b] = P.op("vector", lambda e, hc=hc, b=b: e.tensor_tensor(
                    G[:, hc * 512:(hc + 1) * 512], gt[b][:], ut[b][:], OP.mult),
                    waits=[te1, te2] + ([t_down_last] if (th == 1 and hc < 2) else []),
                    inc="vE")
                tpsum[0 + b] = tgu[b]
                tpsum[2 + b] = tgu[b]
            for ec in range(16):
                b = ec % 2
                if convert and ec not in conv_done["d"]:
                    conv_done["d"].add(ec)
                    tld = P.dma("gpsimd", lambda e, ec=ec, b=b: e.dma_start(
                        out=wdt[b][:].rearrange("p (hc h) -> p hc h", hc=44),
                        in_=d["wqdT"][:, ec * 128:(ec + 1) * 128].rearrange(
                            "(hc p) h -> p hc h", p=128)),
                        "fndl%d" % b, waits=[twfree["d"][b], tst["d"][b]])
                    tts = P.op("vector", lambda e, b=b: e.tensor_scalar(
                        wdt[b][:], wdt[b][:], -128.0, None, OP.add),
                        waits=[tld], inc="vE")
                    tst["d"][b] = P.dma("sync", lambda e, ec=ec, b=b: e.dma_start(
                        out=d["wd_bf"][ec].rearrange("hc p h -> p hc h"),
                        in_=wdt[b][:].rearrange("p (hc h) -> p hc h", hc=44)),
                        "fnds%d" % b, waits=[tts])
                    tw["d"][b] = tts
                else:
                    tw["d"][b] = P.dma("sync", lambda e, ec=ec, b=b: e.dma_start(
                        out=wdt[b][:].rearrange("p (hc h) -> p hc h", hc=44),
                        in_=d["wd_bf"][ec].rearrange("hc p h -> p hc h")),
                        "fndl%d" % b, waits=[twfree["d"][b], tst["d"][b]] + (scrw if ec < 2 else []))
                py = psum["big"][b]
                tm = None
                for hc in range(FFN_HC):
                    tm = P.op("tensor", lambda e, b=b, hc=hc, py=py: e.matmul(
                        py[:], wdt[b][:, hc * 128:(hc + 1) * 128],
                        G[:, hc * 512:(hc + 1) * 512],
                        start=(hc == 0), stop=(hc == FFN_HC - 1)),
                        waits=[tw["d"][b], (tpsum[b] if hc == 0 else None),
                               (tgu[0] if hc == 0 else None),
                               (tgu[1] if hc == 0 else None)],
                        inc=("tE" if hc == FFN_HC - 1 else None))
                twfree["d"][b] = tm
                t_down_last = tm
                tldx = P.dma("sync", lambda e, ec=ec, th=th, b=b: e.dma_start(
                    out=xrt[b][:], in_=d["xres"][ec][:, th * 512:(th + 1) * 512]),
                    "fnx%d" % b, waits=[txr[b]])
                ta = P.op("vector", lambda e, ec=ec, b=b, py=py: e.scalar_tensor_tensor(
                    xrt[b][:], py[:], g["sdown"][:, ec:ec + 1], xrt[b][:],
                    OP.mult, OP.add), waits=[tldx, tm], inc="vE")
                tpsum[b] = ta
                txr[b] = P.dma("sync", lambda e, ec=ec, th=th, b=b: e.dma_start(
                    out=d["xres"][ec][:, th * 512:(th + 1) * 512], in_=xrt[b][:]),
                    "fnxw%d" % b, waits=[ta])
                ctx["t_last"].append(txr[b])
        if convert:
            ctx["t_wscr"] = [t for kind in ("g", "u", "d") for t in tst[kind] if t]


def emit_attn_b(ctx, ab_stop="full"):
    import os
    BULK_VH = int(os.environ.get("AB_BULK_VH", "32"))
    BULK_N = int(os.environ.get("AB_BULK_N", "8"))
    pass


def emit_out(ctx):
    P, nc, d, g, psum = ctx["P"], ctx["nc"], ctx["d"], ctx["g"], ctx["psum"]
    waits = ctx.get("t_last", []) or ctx.get("t_aa", []) or ctx["t_xres"]
    with ExitStack() as sc:
        xc = [sc.enter_context(nc.sbuf_tensor(f"po_x{i}", [128, 128], F32)) for i in range(4)]
        ot = [sc.enter_context(nc.sbuf_tensor(f"po_o{i}", [128, 2048], F32)) for i in range(2)]
        tld = [None] * 4
        tuse = [None] * 4
        tev = [None, None]
        tst = [None, None]
        for tt in range(8):
            ob = tt % 2
            for dc in range(16):
                b = (tt * 16 + dc) % 4
                tld[b] = P.dma("sync", lambda e, tt=tt, dc=dc, b=b: e.dma_start(
                    out=xc[b][:], in_=d["xres"][dc][:, tt * 128:(tt + 1) * 128]),
                    f"pox{b}", waits=[tuse[b]] + (list(waits) if tt == 0 and dc < 4 else []))
                pb = psum["small"][dc % 2]
                ttr = P.op("tensor", lambda e, b=b, pb=pb: e.transpose(
                    pb[:], xc[b][:], g["id_f32"][:]),
                    waits=[tld[b], tev[dc % 2], ctx.get("t_dec_ps")], inc="tE")
                tuse[b] = ttr
                tev[dc % 2] = P.op("vector", lambda e, dc=dc, ob=ob, pb=pb: e.tensor_copy(
                    ot[ob][:, dc * 128:(dc + 1) * 128], pb[:]),
                    waits=[ttr, tst[ob]], inc="vE")
            tst[ob] = P.dma("sync", lambda e, tt=tt, ob=ob: e.dma_start(
                out=d["y"][tt * 128:(tt + 1) * 128, :], in_=ot[ob][:]),
                f"poy{ob}", waits=[tev[0], tev[1]])
        P.op("sync", None, waits=[tst[0], tst[1]])



_PHASES = ("attn_a", "ffn1", "attn_b", "ffn2")


def kernel(L0_attn_a_x, L0_attn_a_w_qkv, L0_attn_a_w_out, L0_attn_a_decay_param,
           L0_attn_b_W_qkvz, L0_attn_b_W_ba, L0_attn_b_conv_w, L0_attn_b_dt_bias,
           L0_attn_b_A_log, L0_attn_b_norm_weight, L0_attn_b_W_out,
           L0_ffn_wq_gate, L0_ffn_s_gate, L0_ffn_wq_up, L0_ffn_s_up,
           L0_ffn_wq_down, L0_ffn_s_down, _phases=None, _ab_stop="full",
           _want_dbg=False):
    phases = _PHASES if _phases is None else _phases
    f32 = np.float32
    x = np.ascontiguousarray(np.asarray(L0_attn_a_x, f32).reshape(8192, 2048))
    shared = {
        "w_qkv": np.ascontiguousarray(np.asarray(L0_attn_a_w_qkv, f32)),
        "w_out_a": np.ascontiguousarray(np.asarray(L0_attn_a_w_out, f32)),
        "decay_p": np.ascontiguousarray(np.asarray(L0_attn_a_decay_param, f32).reshape(16, 1)),
        "wqkvzT": np.ascontiguousarray(np.asarray(L0_attn_b_W_qkvz, f32).T),
        "wbaT": np.ascontiguousarray(np.asarray(L0_attn_b_W_ba, f32).T),
        "woutbT": np.ascontiguousarray(np.asarray(L0_attn_b_W_out, f32).T),
        "conv_w": np.ascontiguousarray(
            np.asarray(L0_attn_b_conv_w, f32).reshape(64, 128, 4).transpose(1, 0, 2).reshape(128, 256)),
        "dt_bias": np.ascontiguousarray(np.asarray(L0_attn_b_dt_bias, f32).reshape(32, 1)),
        "A_log": np.ascontiguousarray(np.asarray(L0_attn_b_A_log, f32).reshape(32, 1)),
        "norm_w": np.ascontiguousarray(np.asarray(L0_attn_b_norm_weight, f32).reshape(128, 1)),
        "wqgT": np.ascontiguousarray(np.asarray(L0_ffn_wq_gate, np.int32).T),
        "wquT": np.ascontiguousarray(np.asarray(L0_ffn_wq_up, np.int32).T),
        "wqdT": np.ascontiguousarray(np.asarray(L0_ffn_wq_down, np.int32).T),
        "s_gate": np.ascontiguousarray(np.asarray(L0_ffn_s_gate, f32).reshape(44, 128).T),
        "s_up": np.ascontiguousarray(np.asarray(L0_ffn_s_up, f32).reshape(44, 128).T),
        "s_down": np.ascontiguousarray(np.asarray(L0_ffn_s_down, f32).reshape(16, 128).T),
    }
    nc = build_nc(phases, _ab_stop)
    in_maps = []
    for c in range(NCORES):
        m = dict(shared)
        m["x_tok"] = np.ascontiguousarray(x[c * 1024:(c + 1) * 1024])
        m.update(host_consts(c))
        in_maps.append(m)
    res = run_bass_kernel_spmd(nc, in_maps, core_ids=list(range(NCORES)))
    if _want_dbg:
        return [np.asarray(r["dbg"]) for r in res.results]
    out = np.concatenate([np.asarray(r["y"]) for r in res.results], axis=0)
    return out.reshape(1, 8192, 2048)


BK_W, BK_NUT, BK_NT, BK_K2, BK_MT, BK_M, BK_SQB = range(7)


def emit_attn_b(ctx, ab_stop="full"):
    import os
    BULK_VH = int(os.environ.get("AB_BULK_VH", "32"))
    BULK_N = int(os.environ.get("AB_BULK_N", "8"))
    P, nc, d, g, psum = ctx["P"], ctx["nc"], ctx["d"], ctx["g"], ctx["psum"]
    prev = ctx.get("t_last") or ctx["t_xres"]
    big, small, pbf = psum["big"], psum["small"], psum["bf"]
    with ExitStack() as sc:
        basb = sc.enter_context(nc.sbuf_tensor("ab_ba", [32, 2048], F32))
        with ExitStack() as sp:
            xln = sp.enter_context(nc.sbuf_tensor("ab_xln", [128, 16384], BF16))
            wqz = [sp.enter_context(nc.sbuf_tensor(f"ab_w{i}", [128, 4096], BF16)) for i in range(2)]
            zb = [sp.enter_context(nc.sbuf_tensor(f"ab_z{i}", [128, 512], BF16)) for i in range(4)]
            wba = sp.enter_context(nc.sbuf_tensor("ab_wba", [128, 1024], BF16))
            halsb = sp.enter_context(nc.sbuf_tensor("ab_hal", [128, 192], BF16))

            tln = emit_ln(ctx, sp, xln, "ln3", prev)
            tw = [tln, tln]
            twfree = [tln, tln]
            tzb = [None] * 4
            tpsum = [tln, tln, tln, tln]
            tqkv = []
            tzw = []
            for js in range(48):
                b = js % 2
                tw[b] = P.dma("gpsimd", lambda e, js=js, b=b: e.dma_start(
                    out=wqz[b][:].rearrange("p (dc j) -> p dc j", dc=16),
                    in_=d["wqkvzT"][:, js * 256:(js + 1) * 256].rearrange(
                        "(dc p) j -> p dc j", p=128)),
                    f"abw{b}", waits=[twfree[b]])
                for jc in range(2):
                    cc = js * 2 + jc
                    for tt in range(2):
                        pp = (cc * 2 + tt) % 4
                        pb = big[pp]
                        tm = None
                        for dc in range(16):
                            tm = P.op("tensor", lambda e, b=b, dc=dc, jc=jc, tt=tt, pb=pb: e.matmul(
                                pb[:], wqz[b][:, dc * 256 + jc * 128: dc * 256 + (jc + 1) * 128],
                                xln[:, dc * 1024 + tt * 512: dc * 1024 + (tt + 1) * 512],
                                start=(dc == 0), stop=(dc == 15)),
                                waits=[tw[b], (tpsum[pp] if dc == 0 else None)],
                                inc=("tE" if dc == 15 else None))
                        twfree[b] = tm
                        zi = (cc * 2 + tt) % 4
                        tev = P.op("vector", lambda e, pb=pb, zi=zi: e.tensor_copy(
                            zb[zi][:], pb[:]), waits=[tm, tzb[zi]], inc="vE")
                        tpsum[pp] = tev
                        if cc < 64:
                            tok = P.dma("sync", lambda e, cc=cc, tt=tt, zi=zi: e.dma_start(
                                out=d["qkvscr"][cc][:, tt * 512:(tt + 1) * 512],
                                in_=zb[zi][:]), f"abq{zi}", waits=[tev])
                            tqkv.append(tok)
                        else:
                            tok = P.dma("sync", lambda e, cc=cc, tt=tt, zi=zi: e.dma_start(
                                out=d["zscr"][cc - 64][:, tt * 512:(tt + 1) * 512],
                                in_=zb[zi][:]), f"abq{zi}", waits=[tev])
                            tzw.append(tok)
                        tzb[zi] = tok
            twb = P.dma("gpsimd", lambda e: e.dma_start(
                out=wba[:].rearrange("p (dc j) -> p dc j", dc=16),
                in_=d["wbaT"].rearrange("(dc p) j -> p dc j", p=128)),
                "abwba", waits=[tln])
            tba = []
            for half in range(2):
                for tt in range(2):
                    pb = big[tt % 2]
                    tm = None
                    for dc in range(16):
                        tm = P.op("tensor", lambda e, half=half, tt=tt, dc=dc, pb=pb: e.matmul(
                            pb[0:32, :], wba[:, dc * 64 + half * 32: dc * 64 + (half + 1) * 32],
                            xln[:, dc * 1024 + tt * 512: dc * 1024 + (tt + 1) * 512],
                            start=(dc == 0), stop=(dc == 15)),
                            waits=[twb, (tpsum[tt % 2] if dc == 0 else None),
                                   (tba[-1] if (dc == 0 and tba) else None)],
                            inc=("tE" if dc == 15 else None))
                    tev = P.op("vector", lambda e, half=half, tt=tt, pb=pb: e.tensor_copy(
                        basb[0:32, half * 1024 + tt * 512: half * 1024 + (tt + 1) * 512],
                        pb[0:32, :]), waits=[tm], inc="vE")
                    tba.append(tev)
            ctx["t_ba"] = tba[-1]
            tpk = None
            thl = P.dma("sync", lambda e: e.dma_start(
                out=halsb[:].rearrange("p (cc k) -> p cc k", cc=64),
                in_=d["qkvscr"][:, :, 1021:1024].rearrange("cc p k -> p cc k")),
                "abhal", waits=tqkv[-4:])
            thw = P.dma("gpsimd", lambda e: e.dma_start(out=d["hal"].ap(), in_=halsb[:]),
                        "abhalw", waits=[thl])
            tag2 = P.op("gpsimd", lambda e: e.collective_compute(
                "AllGather", OP.bypass, replica_groups=[list(range(NCORES))],
                ins=[d["hal"].ap().opt()], outs=[d["halall"].ap().opt()]),
                waits=[thw], inc="ag2")
            ctx["t_qkv"] = tqkv
            ctx["t_zw"] = tzw
            ctx["t_ag2"] = tag2

        if ab_stop == "proj":
            return
        t1c = sc.enter_context(nc.sbuf_tensor("ab_t1", [32, 1024], F32))
        t2c = sc.enter_context(nc.sbuf_tensor("ab_t2", [32, 1024], F32))
        t3c = sc.enter_context(nc.sbuf_tensor("ab_t3", [32, 1024], F32))
        gc_ = sc.enter_context(nc.sbuf_tensor("ab_g", [32, 1024], F32))
        sqs = sc.enter_context(nc.sbuf_tensor("ab_sqs", [32, 1024], F32))
        wA = sc.enter_context(nc.sbuf_tensor("ab_wA", [32, 1024], F32))
        wB = sc.enter_context(nc.sbuf_tensor("ab_wB", [32, 1024], F32))
        wC = sc.enter_context(nc.sbuf_tensor("ab_wC", [32, 1024], F32))
        wD = sc.enter_context(nc.sbuf_tensor("ab_wD", [32, 1024], F32))
        glt = sc.enter_context(nc.sbuf_tensor("ab_glt", [32, 8], F32))
        tsc1 = sc.enter_context(nc.sbuf_tensor("ab_tsc1", [128, 1024], F32))
        tsce = sc.enter_context(nc.sbuf_tensor("ab_tsce", [128, 512], F32))
        egl8 = sc.enter_context(nc.sbuf_tensor("ab_egl8", [32, 8], F32))
        eglc = sc.enter_context(nc.sbuf_tensor("ab_egl", [128, 256], F32))


        with ExitStack() as sp:
            cvt = [sp.enter_context(nc.sbuf_tensor(f"ab_cv{i}", [128, 1027], BF16)) for i in range(2)]
            cvo = [sp.enter_context(nc.sbuf_tensor(f"ab_co{i}", [128, 1024], BF16)) for i in range(2)]
            sqb_ = sp.enter_context(nc.sbuf_tensor("ab_sq", [128, 512], BF16))
            dgt = [sp.enter_context(nc.sbuf_tensor(f"ab_dg{i}", [128, 128], BF16)) for i in range(4)]
            hall = sp.enter_context(nc.sbuf_tensor("ab_hall", [128, 1536], BF16))
            hacc = sp.enter_context(nc.sbuf_tensor("ab_hac", [128, 192], BF16))

            thr = P.dma("sync", lambda e: e.dma_start(
                out=hall[:].rearrange("p (r c) -> p r c", r=8),
                in_=d["halall"].ap().rearrange("(r p) c -> p r c", p=128)),
                "abhr", waits=[ctx["t_ag2"]])
            th = P.op("vector", lambda e: e.memset(hacc[:], 0.0), waits=[thr], inc="vE")
            for j in range(NCORES):
                th = P.op("vector", lambda e, j=j: e.scalar_tensor_tensor(
                    hacc[:], hall[:, j * 192:(j + 1) * 192], g["prevmask"][:, j:j + 1],
                    hacc[:], OP.mult, OP.add), waits=[th], inc="vE")

            tcv = [None, None]
            tuse = [None, None]
            tcst = [None, None]
            tdg = [None] * 4
            tdgr = [None] * 4
            tnq = None
            tsql = None
            tsqr = None
            tnqe = None
            tnke = None
            ctx["t_cvw"] = []
            tqkv_last = ctx["t_qkv"][-4:]
            for cc in range(64):
                b = cc % 2
                tcv[b] = P.dma("sync", lambda e, cc=cc, b=b: e.dma_start(
                    out=cvt[b][:, 3:1027], in_=d["qkvscr"][cc]), f"abcl{b}",
                    waits=[tuse[b]] + ((tqkv_last + [ctx["t_ba"]]) if cc < 2 else []))
                thc = P.op("vector", lambda e, cc=cc, b=b: e.tensor_copy(
                    cvt[b][:, 0:3], hacc[:, cc * 3:(cc + 1) * 3]),
                    waits=[th, tcv[b]], inc="vE")
                for tap in range(4):
                    tdg[tap] = P.op("vector", lambda e, cc=cc, tap=tap: e.tensor_scalar(
                        dgt[tap][:], g["id_bf"][:], g["convw"][:, cc * 4 + tap: cc * 4 + tap + 1],
                        None, OP.mult), waits=[tdgr[tap], ctx["tconst"]], inc="vE")
                tsil = [None, None]
                for tt in range(2):
                    pb = big[2 + (tt % 2)]
                    tm = None
                    for tap in range(4):
                        tm = P.op("tensor", lambda e, b=b, tt=tt, tap=tap, pb=pb: e.matmul(
                            pb[:], dgt[tap][:], cvt[b][:, tt * 512 + tap: tt * 512 + tap + 512],
                            start=(tap == 0), stop=(tap == 3)),
                            waits=[tdg[tap], thc, (tsil[tt] or tcst[b]),
                                   (tpsum[2 + tt] if cc < 2 and tap == 0 else None)],
                            inc=("tE" if tap == 3 else None))
                        tdgr[tap] = tm
                    tev = P.op("scalar", lambda e, b=b, tt=tt, pb=pb: e.activation(
                        cvo[b][:, tt * 512:(tt + 1) * 512], pb[:], AF.Silu),
                        waits=[tm, tcst[b]], inc="sE")
                    tsil[tt] = tev
                    if cc < 32:
                        kh = cc % 16
                        tsql = P.op("scalar", lambda e, b=b, tt=tt: e.activation(
                            sqb_[:], cvo[b][:, tt * 512:(tt + 1) * 512], AF.Square),
                            waits=[tev, tsqr], inc="sE")
                        tnq = P.op("tensor", lambda e, kh=kh, tt=tt, pn=big[tt]: e.matmul(
                            pn[0:32, :], g["indpair16"][:, kh * 32:(kh + 1) * 32],
                            sqb_[:], start=(kh == 0), stop=(kh == 15)),
                            waits=[tsql, tnq] + ([ctx["t_ba"]] if cc == 0 else [])
                                  + ([tnqe] if cc == 16 else []),
                            inc="tE")
                        tsqr = tnq
                if cc == 15:
                    for tt in range(2):
                        tnqe = P.op("vector", lambda e, tt=tt: e.tensor_copy(
                            wA[:, tt * 512:(tt + 1) * 512], big[tt][0:32, :]),
                            waits=[tnq, tnqe], inc="vE")
                if cc == 31:
                    for tt in range(2):
                        tnke = P.op("vector", lambda e, tt=tt: e.tensor_copy(
                            wB[:, tt * 512:(tt + 1) * 512], big[tt][0:32, :]),
                            waits=[tnq, tnke], inc="vE")
                tuse[b] = tsil[1]
                tok = P.dma("sync", lambda e, cc=cc, b=b: e.dma_start(
                    out=d["qkvscr"][cc], in_=cvo[b][:]), f"abcw{b}",
                    waits=[tsil[0], tsil[1]])
                ctx["t_cvw"].append(tok)
                tcst[b] = tok
            ctx["t_nq"] = tnqe
            ctx["t_nk"] = tnke

        LNSCALE = float(np.log(DK ** -0.5))
        bb = basb[0:32, 0:1024]
        aa_ = basb[0:32, 1024:2048]
        t = P.op("scalar", lambda e: e.activation(wD[:], wB[:], AF.Sqrt),
                 waits=[ctx["t_nk"], ctx["t_ba"]], inc="sE")
        t = P.op("vector", lambda e: e.tensor_scalar(wD[:], wD[:], EPS_L2, None, OP.add),
                 waits=[t], inc="vE")
        t = P.op("vector", lambda e: e.reciprocal(wB[:], wD[:]), waits=[t], inc="vE")
        t = P.op("scalar", lambda e: e.activation(wD[:], wB[:], AF.Ln), waits=[t], inc="sE")
        t = P.op("scalar", lambda e: e.activation(wC[:], wA[:], AF.Sqrt),
                 waits=[t, ctx["t_nq"]], inc="sE")
        t = P.op("vector", lambda e: e.tensor_scalar(wC[:], wC[:], EPS_L2, None, OP.add),
                 waits=[t], inc="vE")
        t = P.op("vector", lambda e: e.reciprocal(wA[:], wC[:]), waits=[t], inc="vE")
        t = P.op("scalar", lambda e: e.activation(wC[:], wA[:], AF.Ln), waits=[t], inc="sE")
        t = P.op("vector", lambda e: e.tensor_scalar(wA[:], aa_, g["smallp"][0:32, 1:2],
                 None, OP.add), waits=[t], inc="vE")
        t = P.op("scalar", lambda e: e.activation(gc_[:], wA[:], AF.Abs), waits=[t], inc="sE")
        t = P.op("scalar", lambda e: e.activation(gc_[:], gc_[:], AF.Exp, scale=-1.0),
                 waits=[t], inc="sE")
        t = P.op("scalar", lambda e: e.activation(gc_[:], gc_[:], AF.Ln, bias=1.0),
                 waits=[t], inc="sE")
        t = P.op("scalar", lambda e: e.activation(t1c[:], wA[:], AF.Relu), waits=[t], inc="sE")
        t = P.op("vector", lambda e: e.tensor_tensor(gc_[:], gc_[:], t1c[:], OP.add),
                 waits=[t], inc="vE")
        t = P.op("scalar", lambda e: e.activation(g["smallp"][0:32, 3:4],
                 g["smallp"][0:32, 2:3], AF.Exp), waits=[t], inc="sE")
        t = P.op("vector", lambda e: e.tensor_scalar(gc_[:], gc_[:],
                 g["smallp"][0:32, 3:4], -1.0, OP.mult, OP.mult), waits=[t], inc="vE")
        t = P.op("vector", lambda e: e.memset(wA[:], 1.0), waits=[t], inc="vE")
        t = P.op("vector", lambda e: e.memset(wA[:, 0:1024:CHUNK], 0.0), waits=[t], inc="vE")
        t = P.op("vector", lambda e: e.tensor_tensor_scan(gc_[:], wA[:], gc_[:], 0.0,
                 OP.mult, OP.add), waits=[t], inc="vE")
        tg = t
        t = P.op("vector", lambda e: e.tensor_copy(glt[:], gc_[0:32, CHUNK - 1::CHUNK]),
                 waits=[tg], inc="vE")
        t = P.op("scalar", lambda e: e.activation(egl8[:], glt[:], AF.Exp),
                 waits=[t], inc="sE")
        tegl8 = t
        tb_ = P.op("scalar", lambda e: e.activation(wA[:], bb, AF.Sigmoid),
                   waits=[tg], inc="sE")
        t = P.op("scalar", lambda e: e.activation(t1c[:], wA[:], AF.Ln),
                 waits=[tb_], inc="sE")
        t = P.op("vector", lambda e: e.tensor_tensor(t1c[:], t1c[:], wD[:], OP.add),
                 waits=[t], inc="vE")
        tt1 = P.op("vector", lambda e: e.tensor_tensor(t1c[:], t1c[:], gc_[:], OP.add),
                   waits=[t], inc="vE")
        tt2 = P.op("vector", lambda e: e.tensor_tensor(t2c[:], wD[:], gc_[:], OP.subtract),
                   waits=[tt1], inc="vE")
        t = P.op("vector", lambda e: e.tensor_tensor(t3c[:], gc_[:], wC[:], OP.add),
                 waits=[tt2], inc="vE")
        tt3 = P.op("vector", lambda e: e.tensor_scalar(t3c[:], t3c[:], LNSCALE, None,
                   OP.add), waits=[t], inc="vE")
        tsq = P.op("scalar", lambda e: e.activation(sqs[:], t3c[:], AF.Exp),
                   waits=[tt3], inc="sE")
        tt2p = P.op("vector", lambda e: e.tensor_tensor(
            wB[:], t2c[:], glt[0:32, :].unsqueeze(2).broadcast_to((32, 8, 128)),
            OP.add), waits=[tt2, tegl8], inc="vE")
        tscT = []
        for n in range(NCH):
            for qi, (src, tok) in enumerate([(t1c, tt1), (t2c, tt2), (t3c, tt3), (wA, tb_)]):
                ttr = P.op("tensor", lambda e, n=n, src=src: e.transpose(
                    small[0][:, 0:32], src[0:32, n * 128:(n + 1) * 128],
                    g["id_f32"][0:32, 0:32]),
                    waits=[tok, tt2p] + tscT[-2:], inc="tE")
                tev = P.op("vector", lambda e, n=n, qi=qi: e.tensor_copy(
                    tsc1[:, n * 128 + qi * 32: n * 128 + (qi + 1) * 32],
                    small[0][:, 0:32]), waits=[ttr], inc="vE")
                if qi == 0:
                    tex = P.op("scalar", lambda e, n=n: e.activation(
                        tsce[:, n * 64: n * 64 + 32], small[0][:, 0:32], AF.Exp),
                        waits=[ttr], inc="sE")
                    tscT.append(tev)
                    tscT.append(tex)
                else:
                    tscT.append(tev)
            ttr = P.op("tensor", lambda e, n=n: e.transpose(
                small[0][:, 0:32], wB[0:32, n * 128:(n + 1) * 128],
                g["id_f32"][0:32, 0:32]), waits=[tt2p] + tscT[-2:], inc="tE")
            tex = P.op("scalar", lambda e, n=n: e.activation(
                tsce[:, n * 64 + 32: n * 64 + 64], small[0][:, 0:32], AF.Exp),
                waits=[ttr], inc="sE")
            tscT.append(tex)
        t_tsc = [t for t in tscT[-3:]]
        tegl = None
        for vh in range(HV):
            tm = P.op("tensor", lambda e, vh=vh: e.matmul(
                small[1][:, 0:8], g["ind32_f32"][:, vh:vh + 1].broadcast_to((32, 128)),
                egl8[0:32, :], start=True, stop=True),
                waits=[tegl8, tegl], inc="tE")
            tegl = P.op("vector", lambda e, vh=vh: e.tensor_copy(
                eglc[:, vh * 8:(vh + 1) * 8], small[1][:, 0:8]), waits=[tm], inc="vE")
        ctx["t_prep"] = list(t_tsc) + [tegl, tsq]

        if ab_stop == "conv":
            return
        with ExitStack() as sp:
            qkn = [sp.enter_context(nc.sbuf_tensor(f"ab_qk{i}", [128, 4096], BF16)) for i in range(2)]
            vtn = [sp.enter_context(nc.sbuf_tensor(f"ab_vt{i}", [128, 4096], BF16)) for i in range(2)]
            gkbn = sp.enter_context(nc.sbuf_tensor("ab_gkb", [128, 128], BF16))
            gqkb = sp.enter_context(nc.sbuf_tensor("ab_gqk", [128, 128], BF16))
            Epre = sp.enter_context(nc.sbuf_tensor("ab_Epre", [128, 512], F32))
            Ebf = sp.enter_context(nc.sbuf_tensor("ab_Ebf", [128, 512], BF16))
            dsb = sp.enter_context(nc.sbuf_tensor("ab_D", [128, 128], BF16))
            dtsb = sp.enter_context(nc.sbuf_tensor("ab_DT", [128, 128], BF16))
            ltsb = sp.enter_context(nc.sbuf_tensor("ab_LT", [128, 128], BF16))
            zsb = sp.enter_context(nc.sbuf_tensor("ab_Z", [128, 128], BF16))
            pw = [sp.enter_context(nc.sbuf_tensor(f"ab_pw{i}", [128, 128], BF16)) for i in range(8)]
            rsb = sp.enter_context(nc.sbuf_tensor("ab_R", [128, 256], BF16))
            usb = [sp.enter_context(nc.sbuf_tensor(f"ab_u{i}", [128, 256], BF16)) for i in range(3)]
            lsb = sp.enter_context(nc.sbuf_tensor("ab_l", [128, 256], BF16))
            uu = sp.enter_context(nc.sbuf_tensor("ab_uu", [128, 128], BF16))
            rec = sp.enter_context(nc.sbuf_tensor("ab_rec", [128, 896], BF16))

            REC = lambda i: rec[:, i * 128:(i + 1) * 128]
            tld = [None, None]
            tqk_use = [None, None]
            tvt_use = [None, None]
            trec = None
            trecs = []
            tE = None
            tEx = None
            tA = [None] * 4
            tgk = None
            tgev = None
            tkT = None
            tkTr = None
            tsolve = {}
            tpw = [None] * 8
            tchain = None
            tprev_small = [None, None]
            trv = None
            tw_ = None
            tu_ = None
            tprep = ctx["t_prep"]
            cvw = ctx["t_cvw"]
            for n in range(BULK_N):
                b = n % 2
                tld[b] = P.dma("sync", lambda e, n=n, b=b: e.dma_start(
                    out=qkn[b][:].rearrange("p (cc c) -> p cc c", cc=32),
                    in_=d["qkvscr"][0:32, :, n * 128:(n + 1) * 128].rearrange("cc p c -> p cc c")),
                    f"abbq{b}", waits=[tqk_use[b]] + (cvw[-2:] if n < 2 else []))
                tldv = P.dma("sync", lambda e, n=n, b=b: e.dma_start(
                    out=vtn[b][:].rearrange("p (cc c) -> p cc c", cc=32),
                    in_=d["qkvscr"][32:64, :, n * 128:(n + 1) * 128].rearrange("cc p c -> p cc c")),
                    f"abbv{b}", waits=[tvt_use[b]] + (cvw[-2:] if n < 2 else []))
                for vh in range(BULK_VH):
                    kh = vh // 2
                    ksl = qkn[b][:, (16 + kh) * 128:(17 + kh) * 128]
                    qsl = qkn[b][:, kh * 128:(kh + 1) * 128]
                    if vh % 2 == 0:
                        tm1 = P.op("tensor", lambda e, ksl=ksl: e.matmul(
                            big[3][:, 0:128], ksl, ksl, start=True, stop=True),
                            waits=[tld[b], tgev], inc="tE")
                        tm2 = P.op("tensor", lambda e, ksl=ksl, qsl=qsl: e.matmul(
                            big[3][:, 128:256], ksl, qsl, start=True, stop=True),
                            waits=[tld[b]], inc="tE")
                        tgk = tm2
                        tg1 = P.op("vector", lambda e: e.tensor_scalar(
                            gkbn[:], big[3][:, 0:128], -1.0, None, OP.mult),
                            waits=[tm1, tm2, tA[0], tA[1], tA[2]], inc="vE")
                        tgev = P.op("vector", lambda e: e.tensor_copy(
                            gqkb[:], big[3][:, 128:256]), waits=[tm2, tA[3]], inc="vE")
                        tkT = P.op("tensor", lambda e, ksl=ksl: e.transpose(
                            pbf[1][:], ksl, g["id_bf"][:]), waits=[tld[b], tkTr], inc="tE")
                    colsrc = [t2c, t1c, t1c, t3c]
                    rowg = [0, 1, 1, 1]
                    tcbs = []
                    for q in range(4):
                        tcb = P.op("tensor", lambda e, q=q, n=n, vh=vh, cs=colsrc[q]: e.matmul(
                            big[0][:, q * 128:(q + 1) * 128],
                            g["ind32_f32"][:, vh:vh + 1].broadcast_to((32, 128)),
                            cs[0:32, n * 128:(n + 1) * 128], start=True, stop=True),
                            waits=[tE] + (tprep if (n == 0 and vh == 0) else []), inc="tE")
                        tcbs.append(tcb)
                    tstt = []
                    for q in range(4):
                        ts_ = P.op("vector", lambda e, q=q, n=n, vh=vh, rg=rowg[q]: e.scalar_tensor_tensor(
                            Epre[:, q * 128:(q + 1) * 128], big[0][:, q * 128:(q + 1) * 128],
                            tsc1[:, n * 128 + rg * 32 + vh: n * 128 + rg * 32 + vh + 1],
                            g["masksE"][:, q * 128:(q + 1) * 128], OP.add, OP.add),
                            waits=[tcbs[-1], tEx], inc="vE")
                        tstt.append(ts_)
                    tE = tstt[-1]
                    tEx = P.op("scalar", lambda e: e.activation(Ebf[:], Epre[:], AF.Exp),
                               waits=tstt, inc="sE")
                    tA[0] = P.op("vector", lambda e: e.tensor_tensor(
                        dsb[:], Ebf[:, 0:128], gkbn[:], OP.mult),
                        waits=[tEx, tg1, tpw[0]], inc="vE")
                    tA[1] = P.op("gpsimd", lambda e: e.tensor_tensor(
                        dtsb[:], Ebf[:, 128:256], gkbn[:], OP.mult),
                        waits=[tEx, tg1, tpw[1]], inc="gE")
                    tA[2] = P.op("vector", lambda e: e.tensor_tensor(
                        ltsb[:], Ebf[:, 256:384], gkbn[:], OP.mult),
                        waits=[tEx, tg1, tsolve.get("l3")], inc="vE")
                    tA[3] = P.op("gpsimd", lambda e: e.tensor_tensor(
                        REC(BK_NT), Ebf[:, 384:512], gqkb[:], OP.mult),
                        waits=[tEx, tgev, trec], inc="gE")
                    plist = [dsb, None, None, None, None]
                    qlist = [dtsb, None, None, None, None]
                    tpq = [tA[0], tA[1]]
                    for j in range(1, 5):
                        ps = small[0]
                        tmp_ = P.op("tensor", lambda e, j=j, pl=plist[j - 1], ql=qlist[j - 1]: e.matmul(
                            ps[:], ql[:], pl[:], start=True, stop=True),
                            waits=[tpq[0], tpq[1], tprev_small[0]], inc="tE")
                        t1_ = P.op("vector", lambda e, j=j: e.tensor_copy(
                            pw[2 * (j - 1)][:], ps[:]), waits=[tmp_], inc="vE")
                        ps2 = small[1]
                        tmq_ = P.op("tensor", lambda e, j=j, pl=plist[j - 1], ql=qlist[j - 1]: e.matmul(
                            ps2[:], pl[:], ql[:], start=True, stop=True),
                            waits=[tpq[0], tpq[1], tprev_small[1], tmp_], inc="tE")
                        t2_ = P.op("vector", lambda e, j=j: e.tensor_copy(
                            pw[2 * (j - 1) + 1][:], ps2[:]), waits=[tmq_], inc="vE")
                        plist[j] = pw[2 * (j - 1)]
                        qlist[j] = pw[2 * (j - 1) + 1]
                        tpq = [t1_, t2_]
                        tprev_small = [t1_, t2_]
                        tpw[2 * (j - 1)] = t1_
                        tpw[2 * (j - 1) + 1] = t2_
                    tz = P.op("vector", lambda e: e.tensor_tensor(
                        zsb[:], dtsb[:], g["id_bf"][:], OP.add),
                        waits=[tA[1], tsolve.get("zfree")], inc="vE")
                    for j in range(1, 5):
                        ps = small[0]
                        tmz = P.op("tensor", lambda e, j=j, pl=plist[j]: e.matmul(
                            ps[:], pl[:], zsb[:], start=True, stop=True),
                            waits=[tz, tpw[2 * (j - 1)], tprev_small[0]], inc="tE")
                        tz = P.op("vector", lambda e: e.tensor_tensor(
                            zsb[:], zsb[:], ps[:], OP.add), waits=[tmz], inc="vE")
                        tprev_small[0] = tz
                    tvT = P.op("tensor", lambda e, vh=vh, b=b: e.transpose(
                        pbf[0][:], vtn[b][:, vh * 128:(vh + 1) * 128], g["id_bf"][:]),
                        waits=[tldv, trv, tu_], inc="tE")
                    trv = P.op("vector", lambda e, n=n, vh=vh: e.tensor_scalar(
                        rsb[:, 0:128], pbf[0][:],
                        tsc1[:, n * 128 + 96 + vh: n * 128 + 96 + vh + 1], None, OP.mult),
                        waits=[tvT, tsolve.get("y1")], inc="vE")
                    trk = P.op("vector", lambda e, n=n, vh=vh: e.tensor_scalar(
                        rsb[:, 128:256], pbf[1][:],
                        tsce[:, n * 64 + vh: n * 64 + vh + 1], None, OP.mult),
                        waits=[tkT, tsolve.get("y1")], inc="vE")
                    tk2 = P.op("vector", lambda e, n=n, vh=vh: e.tensor_scalar(
                        REC(BK_K2), pbf[1][:],
                        tsce[:, n * 64 + 32 + vh: n * 64 + 32 + vh + 1], None, OP.mult),
                        waits=[tkT, trec], inc="vE")
                    tkTr = tk2
                    ty1 = P.op("tensor", lambda e: e.matmul(
                        big[1][:, 0:256], zsb[:], rsb[:], start=True, stop=False),
                        waits=[tz, trv, trk, tsolve.get("ydone")], inc="tE")
                    tu0 = P.op("vector", lambda e: e.tensor_copy(usb[0][:], big[1][:, 0:256]),
                               waits=[ty1], inc="vE")
                    tyl = ty1
                    tun = tu0
                    tbig2 = tsolve.get("big2")
                    for k in range(1, 4):
                        tl_ = P.op("tensor", lambda e, k=k: e.matmul(
                            big[2][:, 0:256], ltsb[:], usb[k - 1][:],
                            start=True, stop=True), waits=[tun, tbig2, tA[2]], inc="tE")
                        tle = P.op("vector", lambda e, k=k: e.tensor_copy(
                            lsb[:], big[2][:, 0:256]), waits=[tl_], inc="vE")
                        tym = P.op("tensor", lambda e, k=k: e.matmul(
                            big[1][:, 0:256], zsb[:], lsb[:], start=False, stop=(k == 3)),
                            waits=[tle, tu0 if k == 1 else None], inc="tE")
                        tyl = tym
                        if k < 3:
                            tum = P.op("tensor", lambda e, k=k: e.matmul(
                                big[2][:, 0:256], zsb[:], lsb[:], start=True, stop=True),
                                waits=[tle, tym], inc="tE")
                            tun = P.op("vector", lambda e, k=k: e.tensor_copy(
                                usb[k][:], big[2][:, 0:256]), waits=[tum], inc="vE")
                            tbig2 = tun
                        else:
                            tbig2 = tle
                    tsolve["big2"] = tbig2
                    tw_ = P.op("vector", lambda e: e.tensor_copy(REC(BK_W), big[1][:, 0:128]),
                               waits=[tyl, trec], inc="vE")
                    tu_ = P.op("vector", lambda e: e.tensor_copy(uu[:], big[1][:, 128:256]),
                               waits=[tyl], inc="vE")
                    tsolve["ydone"] = tu_
                    tsolve["y1"] = tu_
                    tsolve["zfree"] = tu_
                    tsolve["l3"] = tbig2
                    tnu = P.op("tensor", lambda e: e.transpose(
                        pbf[0][:], uu[:], g["id_bf"][:]), waits=[tu_, trv], inc="tE")
                    tnue = P.op("vector", lambda e: e.tensor_scalar(
                        REC(BK_NUT), pbf[0][:], -1.0, None, OP.mult),
                        waits=[tnu, trec], inc="vE")
                    tmm_ = P.op("tensor", lambda e: e.matmul(
                        small[1][:], uu[:], REC(BK_K2), start=True, stop=True),
                        waits=[tu_, tk2, tprev_small[1]], inc="tE")
                    tmt = P.op("vector", lambda e, n=n, vh=vh: e.scalar_tensor_tensor(
                        REC(BK_MT), g["id_bf"][:], eglc[:, vh * 8 + n: vh * 8 + n + 1],
                        small[1][:], OP.mult, OP.subtract),
                        waits=[tmm_, tprep[1], trec], inc="vE")
                    tmm2 = P.op("tensor", lambda e: e.matmul(
                        small[1][:], REC(BK_K2), uu[:], start=True, stop=True),
                        waits=[tu_, tk2, tmt], inc="tE")
                    tm_ = P.op("vector", lambda e, n=n, vh=vh: e.scalar_tensor_tensor(
                        REC(BK_M), g["id_bf"][:], eglc[:, vh * 8 + n: vh * 8 + n + 1],
                        small[1][:], OP.mult, OP.subtract),
                        waits=[tmm2, trec], inc="vE")
                    tprev_small[1] = tm_
                    tsqm = P.op("tensor", lambda e, n=n, vh=vh: e.matmul(
                        small[0][:], g["ind32_f32"][:, vh:vh + 1].broadcast_to((32, 128)),
                        sqs[0:32, n * 128:(n + 1) * 128], start=True, stop=True),
                        waits=[tprep[-1], tprev_small[0]], inc="tE")
                    tsqe = P.op("vector", lambda e: e.tensor_copy(REC(BK_SQB), small[0][:]),
                                waits=[tsqm, trec], inc="vE")
                    tprev_small[0] = tsqe
                    tsolve["zfree"] = tmt
                    trec = P.dma("sync", lambda e, n=n, vh=vh: e.dma_start(
                        out=d["bulkscr"][vh][n].rearrange("t p c -> p t c"),
                        in_=rec[:].rearrange("p (t c) -> p t c", t=7)),
                        f"abrec{n % 4}", waits=[tw_, tnue, tmt, tm_, tsqe, tk2, tA[3]])
                    trecs.append(trec)
                    tqk_use[b] = tsqm
                    tvt_use[b] = tvT
            ctx["t_bulk"] = trecs[-4:]

        DUMP = os.environ.get("AB_DUMP", "")
        if DUMP == "qkv":
            P.dma("gpsimd", lambda e: e.dma_start(
                out=d["dbg"], in_=d["qkvscr"]), "dbgd", waits=ctx["t_cvw"][-2:])
        if DUMP == "sc":
            for i, src in enumerate([t1c, t2c, t3c, gc_, sqs]):
                P.dma("gpsimd", lambda e, i=i, src=src: e.dma_start(
                    out=d["dbg"][i][0:32], in_=src[:]), "dbgd", waits=ctx["t_prep"])
            P.dma("gpsimd", lambda e: e.dma_start(
                out=d["dbg"][5][0:32], in_=basb[0:32, 0:1024]), "dbgd", waits=ctx["t_prep"])
            P.dma("gpsimd", lambda e: e.dma_start(
                out=d["dbg"][6][0:32], in_=basb[0:32, 1024:2048]), "dbgd", waits=ctx["t_prep"])
            P.dma("gpsimd", lambda e: e.dma_start(
                out=d["dbg"][7][:, 0:1024], in_=tsc1[:]), "dbgd", waits=ctx["t_prep"])
            P.dma("gpsimd", lambda e: e.dma_start(
                out=d["dbg"][8][:, 0:512], in_=tsce[:]), "dbgd", waits=ctx["t_prep"])
            P.dma("gpsimd", lambda e: e.dma_start(
                out=d["dbg"][9][:, 0:256], in_=eglc[:]), "dbgd", waits=ctx["t_prep"])
        if ab_stop == "bulk":
            if DUMP == "bulk":
                P.dma("gpsimd", lambda e: e.dma_start(
                    out=d["dbg"].rearrange("a p b -> (a p b)")[0:56 * 128 * 128]
                    .rearrange("(a p c) -> a p c", p=128, c=128),
                    in_=d["bulkscr"][0].rearrange("n t p c -> (n t) p c")), "dbgd",
                    waits=ctx["t_bulk"])
            return
        with ExitStack() as sp:
            chset = [sp.enter_context(nc.sbuf_tensor(f"ab_ch{i}", [128, 7168], BF16)) for i in range(2)]
            qh = [sp.enter_context(nc.sbuf_tensor(f"ab_qh{i}", [128, 1024], BF16)) for i in range(2)]
            Ssb = sp.enter_context(nc.sbuf_tensor("ab_S", [128, 128], BF16))
            vns = sp.enter_context(nc.sbuf_tensor("ab_vn", [128, 128], BF16))
            qss = sp.enter_context(nc.sbuf_tensor("ab_qss", [128, 128], BF16))
            outh = sp.enter_context(nc.sbuf_tensor("ab_oh", [128, 1024], BF16))
            mseg = [sp.enter_context(nc.sbuf_tensor(f"ab_ms{i}", [128, 128], BF16)) for i in range(2)]

            CH = lambda vhb, n, t: chset[vhb][:, n * 896 + t * 128:(n) * 896 + (t + 1) * 128]
            tch = [None, None]
            tchu = [None, None]
            tqh = [None, None]
            tqhu = [None, None]
            toh = None
            tpub = []
            tS = None
            tps = [None] * 4
            tqs = None
            tvn = None
            tmsg = [None, None]
            CH_VH = int(os.environ.get("AB_CH_VH", str(HV)))
            CH_STOP = os.environ.get("AB_CH_STOP", "full")
            for vh in range(CH_VH):
                kh = vh // 2
                hb = vh % 2
                if vh % 2 == 0:
                    kb = kh % 2
                    tqh[kb] = P.dma("sync", lambda e, kh=kh, kb=kb: e.dma_start(
                        out=qh[kb][:], in_=d["qkvscr"][kh]), f"abcq{kb}",
                        waits=[tqhu[kb]] + (ctx["t_bulk"] if vh < 2 else []))
                tch[hb] = P.dma("sync", lambda e, vh=vh, hb=hb: e.dma_start(
                    out=chset[hb][:].rearrange("p (nt c) -> p nt c", nt=56),
                    in_=d["bulkscr"][vh].rearrange("n t p c -> p (n t) c")), f"abch{hb}",
                    waits=[tchu[hb]] + (ctx["t_bulk"] if vh < 2 else []))
                tS = P.op("vector", lambda e: e.memset(Ssb[:], 0.0), waits=[tS], inc="vE")
                for n in range(NCH):
                    kb = kh % 2
                    tm1 = P.op("tensor", lambda e, n=n, kb=kb: e.matmul(
                        small[0][:], Ssb[:], qh[kb][:, n * 128:(n + 1) * 128],
                        start=True, stop=True),
                        waits=[tS, tqh[kb], tps[0]], inc="tE")
                    tm2 = P.op("tensor", lambda e, vh=vh, n=n, hb=hb: e.matmul(
                        small[1][:], g["id_bf"][:], CH(hb, n, BK_W),
                        start=True, stop=False), waits=[tch[hb], tps[1]], inc=None)
                    tm2 = P.op("tensor", lambda e, vh=vh, n=n, hb=hb: e.matmul(
                        small[1][:], CH(hb, n, BK_NUT), Ssb[:],
                        start=False, stop=True), waits=[tS], inc="tE")
                    tvn = P.op("vector", lambda e: e.tensor_copy(vns[:], small[1][:]),
                               waits=[tm2], inc="vE")
                    tps[1] = tvn
                    tqs = P.op("vector", lambda e, vh=vh, n=n, hb=hb: e.tensor_tensor(
                        qss[:], small[0][:], CH(hb, n, BK_SQB), OP.mult),
                        waits=[tm1, tqs], inc="vE")
                    tps[0] = tqs
                    tm3 = P.op("tensor", lambda e, vh=vh, n=n, hb=hb: e.matmul(
                        big[0][:, 0:128], vns[:], CH(hb, n, BK_NT),
                        start=True, stop=True), waits=[tvn, tps[2]], inc="tE")
                    toh = P.op("vector", lambda e, n=n: e.tensor_tensor(
                        outh[:, n * 128:(n + 1) * 128], qss[:], big[0][:, 0:128],
                        OP.add), waits=[tm3, tqs, toh], inc="vE")
                    tps[2] = toh
                    tm4 = P.op("tensor", lambda e, vh=vh, n=n, hb=hb: e.matmul(
                        big[1][:, 0:128], CH(hb, n, BK_K2), vns[:],
                        start=True, stop=True), waits=[tvn, tps[3]], inc="tE")
                    tS = P.op("vector", lambda e, vh=vh, n=n: e.scalar_tensor_tensor(
                        Ssb[:], Ssb[:], eglc[:, vh * 8 + n: vh * 8 + n + 1],
                        big[1][:, 0:128], OP.mult, OP.add), waits=[tm4, tS], inc="vE")
                    tps[3] = tS
                if CH_STOP == "loop":
                    toh2 = P.dma("sync", lambda e, vh=vh: e.dma_start(
                        out=d["hscr"][vh], in_=outh[:]), "abhs%d" % (vh % 2), waits=[toh])
                    tpub += [toh2]
                    tchu[hb] = tS
                    tqhu[kh % 2] = tm1
                    tS = toh2
                    toh = toh2
                    tmz = tS
                    continue
                tz = P.op("vector", lambda e, vh=vh, hb=hb: e.tensor_copy(
                    mseg[0][:], CH(hb, 7, BK_MT)), waits=[tch[hb], tmsg[0]], inc="vE")
                zi = 0
                MSEG_ID = os.environ.get("AB_MSEG_ID", "0") == "1"
                MSEG_N = int(os.environ.get("AB_MSEG_N", "7"))
                for n in range(MSEG_N - 1, -1, -1):
                    tmz = P.op("tensor", lambda e, vh=vh, n=n, hb=hb, zi=zi, MI=MSEG_ID: e.matmul(
                        big[2][:, 0:128], (g["id_bf"][:] if MI else CH(hb, n, BK_M)), mseg[zi][:],
                        start=True, stop=True), waits=[tz, tmsg[1 - zi], tps[1]], inc="tE")
                    tps[1] = tmz
                    zi = 1 - zi
                    eng = "vector" if n % 2 == 0 else "scalar"
                    if eng == "vector":
                        tz = P.op("vector", lambda e, zi=zi: e.tensor_copy(
                            mseg[zi][:], big[2][:, 0:128]), waits=[tmz], inc="vE")
                    else:
                        tz = P.op("scalar", lambda e, zi=zi: e.activation(
                            mseg[zi][:], big[2][:, 0:128], AF.Copy), waits=[tmz], inc="sE")
                    tmsg[zi] = tz
                tp1 = P.dma("gpsimd", lambda e, vh=vh, zi=zi: e.dma_start(
                    out=d["pub"].ap()[vh, 0], in_=mseg[zi][:]), f"abpub{vh % 2}",
                    waits=[tz])
                tmsg[zi] = tp1
                tp2 = P.dma("gpsimd", lambda e, vh=vh: e.dma_start(
                    out=d["pub"].ap()[vh, 1], in_=Ssb[:]), f"abpub{vh % 2}",
                    waits=[tS])
                toh2 = P.dma("sync", lambda e, vh=vh: e.dma_start(
                    out=d["hscr"][vh], in_=outh[:]), f"abhs{vh % 2}", waits=[toh])
                tpub += [tp1, tp2, toh2]
                tchu[hb] = tmz
                tqhu[kh % 2] = tm1
                tS = tp2
                toh = toh2

            if CH_STOP != "full":
                return
            tag3 = P.op("gpsimd", lambda e: e.collective_compute(
                "AllGather", OP.bypass, replica_groups=[list(range(NCORES))],
                ins=[d["pub"].ap().opt()], outs=[d["puball"].ap().opt()]),
                waits=tpub, inc="ag3")

        if ab_stop == "chain":
            if DUMP == "hscr":
                P.dma("gpsimd", lambda e: e.dma_start(
                    out=d["dbg"][0:32], in_=d["hscr"]), "dbgd", waits=tpub[-3:])
            return
        with ExitStack() as sp:
            Sall = sp.enter_context(nc.sbuf_tensor("ab_Sall", [128, 4096], BF16))
            pj = [sp.enter_context(nc.sbuf_tensor(f"ab_pj{i}", [128, 4096], BF16)) for i in range(2)]
            msub = sp.enter_context(nc.sbuf_tensor("ab_msub", [128, 128], BF16))
            vall = sp.enter_context(nc.sbuf_tensor("ab_vall", [128, 1024], BF16))
            mts = sp.enter_context(nc.sbuf_tensor("ab_mts", [128, 1024], BF16))

            tsa = P.op("vector", lambda e: e.memset(Sall[:], 0.0), waits=[tag3], inc="vE")
            tpju = [None, None]
            tms = None
            tps0 = None
            for j in range(NCORES - 1):
                for half in range(2):
                    hb = half
                    tpj0 = P.dma("sync", lambda e, j=j, half=half, hb=hb: e.dma_start(
                        out=pj[hb][:, 0:2048].rearrange("p (h c) -> p h c", h=16),
                        in_=d["puball"].ap()[j * 32 + half * 16: j * 32 + (half + 1) * 16, 0]
                        .rearrange("h p c -> p h c")),
                        f"abpj{hb}", waits=[tpju[hb], tag3])
                    tpj = P.dma("sync", lambda e, j=j, half=half, hb=hb: e.dma_start(
                        out=pj[hb][:, 2048:4096].rearrange("p (h c) -> p h c", h=16),
                        in_=d["puball"].ap()[j * 32 + half * 16: j * 32 + (half + 1) * 16, 1]
                        .rearrange("h p c -> p h c")),
                        f"abpj{hb}", waits=[tpj0])
                    for h16 in range(16):
                        h = half * 16 + h16
                        tmsb = P.op("vector", lambda e, h16=h16, hb=hb: e.tensor_tensor(
                            msub[:], pj[hb][:, h16 * 128: h16 * 128 + 128],
                            g["id_bf"][:], OP.subtract), waits=[tpj, tms], inc="vE")
                        tmm1 = P.op("tensor", lambda e, h=h, hb=hb, h16=h16: e.matmul(
                            small[0][:], msub[:], Sall[:, h * 128:(h + 1) * 128],
                            start=True, stop=False), waits=[tmsb, tsa, tps0], inc=None)
                        tmm1 = P.op("tensor", lambda e, h=h, hb=hb, h16=h16: e.matmul(
                            small[0][:], g["id_bf"][:],
                            pj[hb][:, 2048 + h16 * 128: 2048 + h16 * 128 + 128],
                            start=False, stop=True), waits=[], inc="tE")
                        tms = tmm1
                        tsa = P.op("vector", lambda e, j=j, h=h: e.scalar_tensor_tensor(
                            Sall[:, h * 128:(h + 1) * 128], small[0][:],
                            g["coremask"][:, j:j + 1], Sall[:, h * 128:(h + 1) * 128],
                            OP.mult, OP.add), waits=[tmm1, tsa], inc="vE")
                        tps0 = tsa
                    tpju[hb] = tms

            tvw = []
            tmtu = [None, None]
            tvu = None
            for vh in range(HV):
                hb = vh % 2
                tmt = P.dma("sync", lambda e, vh=vh, hb=hb: e.dma_start(
                    out=mts[:].rearrange("p (n c) -> p n c", n=8),
                    in_=d["bulkscr"][vh, :, BK_MT].rearrange("n p c -> p n c")), "abpub%d" % hb, waits=[tmtu[hb]])
                tvc = P.op("vector", lambda e, vh=vh: e.tensor_copy(
                    vall[:, 0:128], Sall[:, vh * 128:(vh + 1) * 128]),
                    waits=[tsa, tvu], inc="vE")
                for n in range(NCH - 1):
                    tvm = P.op("tensor", lambda e, n=n: e.matmul(
                        small[1][:], mts[:, n * 128:(n + 1) * 128], vall[:, n * 128:(n + 1) * 128],
                        start=True, stop=True), waits=[tvc, tmt, tps0], inc="tE")
                    tps0 = tvm
                    tvc = P.op("vector", lambda e, n=n: e.tensor_copy(
                        vall[:, (n + 1) * 128:(n + 2) * 128], small[1][:]),
                        waits=[tvm], inc="vE")
                tvu = tvc
                tmtu[hb] = tvm if NCH > 1 else tmt
                tvw.append(P.dma("sync", lambda e, vh=vh: e.dma_start(
                    out=d["vscr"][vh].rearrange("n p c -> p n c"),
                    in_=vall[:].rearrange("p (n c) -> p n c", n=8)),
                    "abhs%d" % (vh % 2), waits=[tvc]))
                tvu = tvw[-1]
            ctx["t_vw"] = tvw

        if ab_stop == "vchain":
            if DUMP == "sall":
                P.dma("gpsimd", lambda e: e.dma_start(
                    out=d["dbg"][0:4].rearrange("a p c -> p a c"),
                    in_=Sall[:].rearrange("p (a c) -> p a c", a=4)),
                    "dbgd", waits=[tsa])
            if DUMP == "v":
                P.dma("gpsimd", lambda e: e.dma_start(
                    out=d["dbg"][0:32].rearrange("h p (n c) -> h n p c", n=8),
                    in_=d["vscr"]), "dbgd", waits=ctx["t_vw"][-2:])
            return
        with ExitStack() as sp:
            hn = sp.enter_context(nc.sbuf_tensor("ab_hn", [128, 4096], BF16))
            Vn = sp.enter_context(nc.sbuf_tensor("ab_Vn", [128, 4096], BF16))
            c12 = sp.enter_context(nc.sbuf_tensor("ab_c12", [128, 8192], BF16))
            csq = sp.enter_context(nc.sbuf_tensor("ab_csq", [128, 4096], BF16))
            qn = sp.enter_context(nc.sbuf_tensor("ab_qn", [128, 2048], BF16))
            zn = sp.enter_context(nc.sbuf_tensor("ab_zn", [128, 4096], BF16))
            uvs = sp.enter_context(nc.sbuf_tensor("ab_uvs", [128, 128], BF16))
            tsb = sp.enter_context(nc.sbuf_tensor("ab_tsb", [128, 128], BF16))
            sqt = sp.enter_context(nc.sbuf_tensor("ab_sqt", [128, 128], BF16))
            rsq = sp.enter_context(nc.sbuf_tensor("ab_rsq", [32, 256], F32))
            zs = sp.enter_context(nc.sbuf_tensor("ab_zs", [128, 128], BF16))
            hgrow = sp.enter_context(nc.sbuf_tensor("ab_hgr", [128, 4096], BF16))

            tld6 = {}
            tn_use = {}
            ctx["t_hg"] = []
            tps6 = [None] * 6
            tuv = None
            ttsb = None
            tsqt = None
            tzs = None
            thg = None
            trr = None
            for n in range(NCH):
                w0 = [ctx["t_vw"][-2], ctx["t_vw"][-1]] if n < 2 else []
                tl_h = P.dma("sync", lambda e, n=n: e.dma_start(
                    out=hn[:].rearrange("p (h c) -> p h c", h=32),
                    in_=d["hscr"][:, :, n * 128:(n + 1) * 128].rearrange("h p c -> p h c")), "abw%d" % (n % 2),
                    waits=[tn_use.get(("h", n - 1))] + w0)
                tl_v = P.dma("sync", lambda e, n=n: e.dma_start(
                    out=Vn[:].rearrange("p (h c) -> p h c", h=32),
                    in_=d["vscr"][:, n].rearrange("h p c -> p h c")), "abq%d" % (n % 2),
                    waits=[tn_use.get(("v", n - 1))] + w0)
                tl_c0 = P.dma("sync", lambda e, n=n: e.dma_start(
                    out=c12[:, 0:4096].rearrange("p (h c) -> p h c", h=32),
                    in_=d["bulkscr"][:, n, BK_NUT].rearrange("h p c -> p h c")), "abq%d" % (2 + n % 2),
                    waits=[tn_use.get(("c", n - 1))] + w0)
                tl_c = P.dma("sync", lambda e, n=n: e.dma_start(
                    out=c12[:, 4096:8192].rearrange("p (h c) -> p h c", h=32),
                    in_=d["bulkscr"][:, n, BK_NT].rearrange("h p c -> p h c")), "abq%d" % (2 + n % 2),
                    waits=[tl_c0])
                tl_s = P.dma("sync", lambda e, n=n: e.dma_start(
                    out=csq[:].rearrange("p (h c) -> p h c", h=32),
                    in_=d["bulkscr"][:, n, BK_SQB].rearrange("h p c -> p h c")), "abcl%d" % (n % 2),
                    waits=[tn_use.get(("s", n - 1))] + w0)
                tl_q = P.dma("sync", lambda e, n=n: e.dma_start(
                    out=qn[:].rearrange("p (h c) -> p h c", h=16),
                    in_=d["qkvscr"][0:16, :, n * 128:(n + 1) * 128].rearrange("h p c -> p h c")), "abcw%d" % (n % 2),
                    waits=[tn_use.get(("q", n - 1))] + w0)
                tl_z = P.dma("sync", lambda e, n=n: e.dma_start(
                    out=zn[:].rearrange("p (h c) -> p h c", h=32),
                    in_=d["zscr"][:, :, n * 128:(n + 1) * 128].rearrange("h p c -> p h c")), "abhal" if n % 2 == 0 else "abhalw",
                    waits=[tn_use.get(("z", n - 1))] + (ctx["t_zw"][-4:] if n < 2 else []))
                trmsA = None
                for vh in range(HV):
                    NUT = c12[:, vh * 128: vh * 128 + 128]
                    NT = c12[:, 4096 + vh * 128: 4096 + vh * 128 + 128]
                    SQB = csq[:, vh * 128:(vh + 1) * 128]
                    VS = Vn[:, vh * 128:(vh + 1) * 128]
                    HS = hn[:, vh * 128:(vh + 1) * 128]
                    tuvm = P.op("tensor", lambda e, NUT=NUT, VS=VS: e.matmul(
                        small[0][:], NUT, VS, start=True, stop=True),
                        waits=[tl_c, tl_v, tps6[0]], inc="tE")
                    tuv = P.op("vector", lambda e: e.tensor_copy(uvs[:], small[0][:]),
                               waits=[tuvm, tuv, tps6[2]], inc="vE")
                    tps6[0] = tuv
                    tqvm = P.op("tensor", lambda e, VS=VS, vh=vh: e.matmul(
                        small[1][:], VS, qn[:, (vh // 2) * 128: (vh // 2 + 1) * 128],
                        start=True, stop=True), waits=[tl_q, tl_v, tps6[1]], inc="tE")
                    tnvm = P.op("tensor", lambda e, NT=NT: e.matmul(
                        big[0][:, 0:128], uvs[:], NT, start=True, stop=True),
                        waits=[tuv, tps6[2]], inc="tE")
                    ttsb = P.op("vector", lambda e, SQB=SQB: e.tensor_tensor(
                        tsb[:], small[1][:], SQB, OP.mult),
                        waits=[tqvm, tl_s, ttsb], inc="vE")
                    tps6[1] = ttsb
                    ttsb = P.op("vector", lambda e: e.tensor_tensor(
                        tsb[:], tsb[:], big[0][:, 0:128], OP.add), waits=[ttsb, tnvm], inc="vE")
                    tps6[2] = ttsb
                    th_ = P.op("vector", lambda e, HS=HS: e.tensor_tensor(
                        HS, HS, tsb[:], OP.add), waits=[ttsb, tl_h], inc="vE")
                    tsqt = P.op("scalar", lambda e, HS=HS: e.activation(
                        sqt[:], HS, AF.Square), waits=[th_, tsqt, trmsA], inc="sE")
                    trmsA = P.op("tensor", lambda e, vh=vh: e.matmul(
                        big[1][0:32, 0:128], g["indsel32"][:, vh * 32:(vh + 1) * 32],
                        sqt[:], start=(vh == 0), stop=(vh == 31)),
                        waits=[tsqt, trmsA, (tps6[3] if vh == 0 else None)], inc="tE")
                    tn_use[("h", n)] = th_
                if os.environ.get("AB_DUMP") == "hmerge":
                    P.dma("gpsimd", lambda e, n=n: e.dma_start(
                        out=d["dbg"][4 * n:4 * n + 4].rearrange("a p c -> p a c"),
                        in_=hn[:].rearrange("p (a c) -> p a c", a=4)), "dbgd",
                        waits=[tn_use[("h", n)]])
                t = P.op("vector", lambda e: e.tensor_scalar(
                    rsq[0:32, 0:128], big[1][0:32, 0:128], 1.0 / DV, None, OP.mult),
                    waits=[trmsA, trr], inc="vE")
                tps6[3] = t
                t = P.op("scalar", lambda e: e.activation(
                    rsq[0:32, 128:256], rsq[0:32, 0:128], AF.Sqrt, bias=EPS_RMS),
                    waits=[t], inc="sE")
                trr = P.op("vector", lambda e: e.reciprocal(
                    rsq[0:32, 0:128], rsq[0:32, 128:256]), waits=[t], inc="vE")
                thg2 = None
                for vh in range(HV):
                    HS = hn[:, vh * 128:(vh + 1) * 128]
                    tbm = P.op("tensor", lambda e, vh=vh: e.matmul(
                        big[2][:, 0:128], g["ind32_f32"][:, vh:vh + 1].broadcast_to((32, 128)),
                        rsq[0:32, 0:128], start=True, stop=True),
                        waits=[trr, tps6[4]], inc="tE")
                    tzs = P.op("scalar", lambda e, vh=vh: e.activation(
                        zs[:], zn[:, vh * 128:(vh + 1) * 128], AF.Silu),
                        waits=[tl_z, tzs, thg2], inc="sE")
                    t1_ = P.op("vector", lambda e, HS=HS: e.scalar_tensor_tensor(
                        tsb[:], HS, g["norm_w"][:, 0:1], big[2][:, 0:128],
                        OP.mult, OP.mult), waits=[tbm, ttsb, tn_use[("h", n)]], inc="vE")
                    tps6[4] = t1_
                    thg2 = P.op("vector", lambda e, vh=vh: e.tensor_tensor(
                        hgrow[:, vh * 128:(vh + 1) * 128], tsb[:], zs[:], OP.mult),
                        waits=[t1_, tzs, thg], inc="vE")
                    ttsb = thg2
                thw = P.dma("sync", lambda e, n=n: e.dma_start(
                    out=d["hgscr"][:, :, n * 128:(n + 1) * 128].rearrange("h p c -> p h c"),
                    in_=hgrow[:].rearrange("p (h c) -> p h c", h=32)), "abbq%d" % (n % 2),
                    waits=[thg2])
                thg = thw
                ctx["t_hg"].append(thw)
                for k in ("v", "c", "s", "q", "z"):
                    tn_use[(k, n)] = thg2
                tn_use[("h", n)] = thw

        if ab_stop == "corr":
            if DUMP == "hg":
                P.dma("gpsimd", lambda e: e.dma_start(
                    out=d["dbg"][0:32], in_=d["hgscr"]), "dbgd", waits=ctx["t_hg"][-2:])
            return
        with ExitStack() as sp:
            hg = sp.enter_context(nc.sbuf_tensor("ab_hg", [128, 32768], BF16))
            wo = [sp.enter_context(nc.sbuf_tensor(f"ab_wo{i}", [128, 4096], BF16)) for i in range(2)]
            xrt = [sp.enter_context(nc.sbuf_tensor(f"ab_xr{i}", [128, 512], F32)) for i in range(2)]
            thl = P.dma("sync", lambda e: e.dma_start(
                out=hg[:].rearrange("p (h c) -> p h c", h=32),
                in_=d["hgscr"].rearrange("h p c -> p h c")), "abhr",
                waits=ctx["t_hg"][-2:])
            ctx["t_last"] = []
            tw7 = [None, None]
            tw7f = [thl, thl]
            txr = [None, None]
            tps7 = [None, None, None, None]
            for ec in range(16):
                b = ec % 2
                tw7[b] = P.dma("gpsimd", lambda e, ec=ec, b=b: e.dma_start(
                    out=wo[b][:].rearrange("p (v c) -> p v c", v=32),
                    in_=d["woutbT"][:, ec * 128:(ec + 1) * 128].rearrange(
                        "(v p) c -> p v c", p=128)), "abbv%d" % b, waits=[tw7f[b]])
                for tt in range(2):
                    pp = (ec * 2 + tt) % 4
                    pb = big[pp]
                    tm = None
                    for vc in range(32):
                        tm = P.op("tensor", lambda e, b=b, vc=vc, tt=tt, pb=pb: e.matmul(
                            pb[:], wo[b][:, vc * 128:(vc + 1) * 128],
                            hg[:, vc * 1024 + tt * 512: vc * 1024 + (tt + 1) * 512],
                            start=(vc == 0), stop=(vc == 31)),
                            waits=[tw7[b], thl, (tps7[pp] if vc == 0 else None)],
                            inc=("tE" if vc == 31 else None))
                    tw7f[b] = tm
                    xb = (ec * 2 + tt) % 2
                    tldx = P.dma("sync", lambda e, ec=ec, tt=tt, xb=xb: e.dma_start(
                        out=xrt[xb][:], in_=d["xres"][ec][:, tt * 512:(tt + 1) * 512]),
                        "abrec%d" % xb, waits=[txr[xb]])
                    ta = P.op("vector", lambda e, xb=xb, pb=pb: e.tensor_tensor(
                        xrt[xb][:], xrt[xb][:], pb[:], OP.add), waits=[tldx, tm], inc="vE")
                    tps7[pp] = ta
                    txr[xb] = P.dma("sync", lambda e, ec=ec, tt=tt, xb=xb: e.dma_start(
                        out=d["xres"][ec][:, tt * 512:(tt + 1) * 512], in_=xrt[xb][:]),
                        "abrec%d" % (2 + xb), waits=[ta])
                    ctx["t_last"].append(txr[xb])
